# version 02
# baseline (speedup 1.0000x reference)
import contextlib
import numpy as np
import ml_dtypes
import concourse.bass as bass
import concourse.mybir as mybir
from concourse.bass_utils import run_bass_kernel_spmd

F32 = mybir.dt.float32
BF16 = mybir.dt.bfloat16
AF = mybir.ActivationFunctionType
ALU = mybir.AluOpType
AX = mybir.AxisListType

P = 128
D_MODEL = 1024
D_INNER = 2048
DM = 8
EC = 16
R = 64
NST = 16
NL = 4
NE = 8
HD = 2048
TH = 1024
TI = 512
GN = 2
NGH = 4
EPS = 1e-5


class Cnt:
    def __init__(self, h):
        self.h = h
        self.n = 0

    def inc(self, instr, k=1):
        instr.then_inc(self.h, k)
        self.n += k
        return self.n


def build(n_tp, taps=()):
    s_core = n_tp * TH
    nc = bass.Bass("TRN2", target_bir_lowering=False, detect_race_conditions=False)
    taps = set(taps)
    tap_seen = {}

    x_hbm = nc.dram_tensor("x_hbm", [D_MODEL, s_core], F32, kind="ExternalInput").ap()
    y_hbm = nc.dram_tensor("y_hbm", [D_MODEL, s_core], F32, kind="ExternalOutput").ap()
    in_wT = [nc.dram_tensor(f"in_wT{l}", [32, D_MODEL, 128], F32, kind="ExternalInput").ap() for l in range(NL)]
    out_wT = [nc.dram_tensor(f"out_wT{l}", [DM, D_INNER, 128], F32, kind="ExternalInput").ap() for l in range(NL)]
    xp_wT = [nc.dram_tensor(f"xp_wT{l}", [EC, 128, 96], F32, kind="ExternalInput").ap() for l in range(NL)]
    dt_wT = [nc.dram_tensor(f"dt_wT{l}", [EC, R, 128], F32, kind="ExternalInput").ap() for l in range(NL)]
    small = [nc.dram_tensor(f"small{l}", [P, 384], F32, kind="ExternalInput").ap() for l in range(NL)]
    wgT = nc.dram_tensor("wgT", [D_MODEL, 8], F32, kind="ExternalInput").ap()
    w1T = [nc.dram_tensor(f"w1T{e}", [16, D_MODEL, 128], F32, kind="ExternalInput").ap() for e in range(NE)]
    w2T = [nc.dram_tensor(f"w2T{e}", [16, D_MODEL, 128], F32, kind="ExternalInput").ap() for e in range(NE)]
    w3T = [nc.dram_tensor(f"w3T{e}", [DM, HD, 128], F32, kind="ExternalInput").ap() for e in range(NE)]

    with contextlib.ExitStack() as stk:
        E = stk.enter_context
        x_t = E(nc.sbuf_tensor("x_t", [P, DM, TH], BF16))
        z_t = E(nc.sbuf_tensor("z_t", [P, EC, TH], BF16))
        xc_t = E(nc.sbuf_tensor("xc_t", [P, EC, TH], BF16))
        xin_c = E(nc.sbuf_tensor("xin_c", [P, 2, TH + 3], BF16))
        dA_t = E(nc.sbuf_tensor("dA_t", [P, 2, GN, TI], F32))
        brep = E(nc.sbuf_tensor("brep", [P, 8, TI], BF16))
        crep = E(nc.sbuf_tensor("crep", [P, 8, TI], BF16))
        xn_t = E(nc.sbuf_tensor("xn_t", [P, DM, TH], BF16))
        y_g = E(nc.sbuf_tensor("y_g", [P, EC, TI], BF16))
        b_t = E(nc.sbuf_tensor("b_t", [P, GN, TI], BF16))
        h_t = E(nc.sbuf_tensor("h_t", [P, GN, TI], BF16))
        hc_t = E(nc.sbuf_tensor("hc_t", [P, GN, TI], BF16))
        ya_t = E(nc.sbuf_tensor("ya_t", [P, 2, TI], BF16))
        yt_t = E(nc.sbuf_tensor("yt_t", [P, 2, TI], BF16))
        mubb, rsbb = None, None
        dts = E(nc.sbuf_tensor("dts", [P, 6, TI], F32))
        ost = E(nc.sbuf_tensor("ost", [P, 2, TI], F32))
        wst = E(nc.sbuf_tensor("wst", [P, 1024], F32))
        wbf = E(nc.sbuf_tensor("wbf", [P, 2, 2048], BF16))
        dtrb = E(nc.sbuf_tensor("dtrb", [R, TH], BF16))
        dtwb = E(nc.sbuf_tensor("dtwb", [R, EC * 128], BF16))
        brow = E(nc.sbuf_tensor("brow", [NST, TH], BF16))
        crow = E(nc.sbuf_tensor("crow", [NST, TH], BF16))
        smt = E(nc.sbuf_tensor("smt", [P, NL, 384], F32))
        carry = E(nc.sbuf_tensor("carry", [P, NL, EC, NST], F32))
        halo = E(nc.sbuf_tensor("halo", [P, NL, EC, 3], BF16))
        idn = E(nc.sbuf_tensor("idn", [P, P], F32))
        ones = E(nc.sbuf_tensor("ones", [P, 2], F32))
        onesb = E(nc.sbuf_tensor("onesb", [P, 2], BF16))
        onesr = E(nc.sbuf_tensor("onesr", [1, P], F32))
        ioc = E(nc.sbuf_tensor("ioc", [P, 1], F32))
        diag = E(nc.sbuf_tensor("diag", [P, 4, P], BF16))
        sel = E(nc.sbuf_tensor("sel", [NST, NST, P], BF16))
        gt_t = E(nc.sbuf_tensor("gt_t", [P, DM, 10], F32))
        wtok = E(nc.sbuf_tensor("wtok", [P, DM, 8], F32))
        rt_t = E(nc.sbuf_tensor("rt_t", [P, 4, 8], F32))
        gg = E(nc.sbuf_tensor("gg", [P, 8, 8], F32))
        ggm = E(nc.sbuf_tensor("ggm", [P, 4, 8], F32))
        wfm = E(nc.sbuf_tensor("wfm", [8, DM * P], BF16))

        ps = [E(nc.psum_tensor(f"ps{i}", [P, TI], F32)) for i in range(8)]

        mubb, rsbb = ya_t, yt_t
        s_dma = Cnt(E(nc.semaphore("s_dma")))
        s_sm = Cnt(E(nc.semaphore("s_sm")))
        s_xd = [Cnt(E(nc.semaphore("s_xd0"))), Cnt(E(nc.semaphore("s_xd1")))]
        s_ou = [Cnt(E(nc.semaphore("s_ou0"))), Cnt(E(nc.semaphore("s_ou1")))]
        s_tap = Cnt(E(nc.semaphore("s_tap")))
        s_pe = Cnt(E(nc.semaphore("s_pe")))
        s_act = Cnt(E(nc.semaphore("s_act")))
        s_dve = Cnt(E(nc.semaphore("s_dve")))
        s_pool = Cnt(E(nc.semaphore("s_pool")))
        block = E(nc.Block())

        SY, GP, TE, VE, SC = block.sync, block.gpsimd, block.tensor, block.vector, block.scalar

        free = {}

        def need(eng, *keys):
            for key in keys:
                for sem, val in free.pop(key, ()):
                    eng.wait_ge(sem.h, val)

        def rel(key, *sems):
            free[key] = [(s, s.n) for s in sems]

        def dbg(name, ap):
            if name not in taps or name in tap_seen:
                return
            t = nc.dram_tensor("dbg_" + name, list(ap.shape), ap.dtype,
                               kind="ExternalOutput").ap()
            tap_seen[name] = True
            pe_m, act_m, dve_m, pool_m = s_pe.n, s_act.n, s_dve.n, s_pool.n

            def f(gpsimd):
                gpsimd.wait_ge(s_pe.h, pe_m)
                gpsimd.wait_ge(s_act.h, act_m)
                gpsimd.wait_ge(s_dve.h, dve_m)
                gpsimd.wait_ge(s_pool.h, pool_m)
                s_tap.inc(gpsimd.dma_start(out=t, in_=ap), 16)
            GP(f)
            tv = s_tap.n

            def f2(eng):
                eng.wait_ge(s_tap.h, tv)
            TE(f2)
            VE(f2)
            SC(f2)
            SY(f2)

        def lnw(l):
            return smt[:, l, 0:8]

        def lnb(l):
            return smt[:, l, 8:16]

        def cwv(l):
            return smt[:, l, 16:80].rearrange("p (c k) -> p c k", c=EC)

        def cbv(l):
            return smt[:, l, 80:96]

        def dtb(l):
            return smt[:, l, 96:112]

        def dpv(l):
            return smt[:, l, 112:128]

        def A_v(l):
            return smt[:, l, 128:384].rearrange("p (c n) -> p c n", c=EC)

        dts_flat = dts[:].rearrange("p a t -> p (a t)")
        mu_row = dts_flat[0:1, 0:TH]
        m2_row = dts_flat[0:1, TH:2 * TH]
        var_row = dts_flat[0:1, 2 * TH:3 * TH]
        rs_row = var_row

        def f(sync):
            for l in range(NL):
                s_sm.inc(sync.dma_start(out=smt[:, l, :], in_=small[l]), 16)
        SY(f)
        sm_ld = s_sm.n

        def f(gpsimd):
            gpsimd.iota(idn[:], pattern=[[1, P]], base=0, channel_multiplier=0,
                        allow_small_or_imprecise_dtypes=True)
            gpsimd.iota(ioc[:], pattern=[[0, 1]], base=0, channel_multiplier=1,
                        allow_small_or_imprecise_dtypes=True)
            gpsimd.memset(carry[:], 0.0)
            gpsimd.memset(ones[:], 1.0)
            gpsimd.memset(onesb[:], 1.0)
            s_pool.inc(gpsimd.memset(onesr[:], 1.0))
        GP(f)
        pool_init = s_pool.n

        def f(vector):
            vector.wait_ge(s_pool.h, pool_init)
            vector.tensor_tensor(idn[:], idn[:], ioc[:].broadcast_to([P, P]),
                                 op=ALU.is_equal)
            for n in range(NST):
                vector.tensor_scalar(sel[:, n, :], ioc[0:NST, :].broadcast_to([NST, P]),
                                     float(n), None, op0=ALU.is_equal)
            s_dve.inc(vector.memset(halo[:], 0.0))
        VE(f)
        init_done = s_dve.n

        def f(scalar):
            scalar.wait_ge(s_sm.h, sm_ld)
            for l in range(NL):
                s_act.inc(scalar.activation(smt[:, l, 128:384], smt[:, l, 128:384], AF.Exp))
        SC(f)
        act_A = s_act.n

        def f(vector):
            vector.wait_ge(s_act.h, act_A)
            for l in range(NL):
                s_dve.inc(vector.tensor_scalar(smt[:, l, 128:384], smt[:, l, 128:384],
                                               -1.0, None, op0=ALU.mult))
        VE(f)

        def layernorm(src, l, apply_w, dst, to_ost=False, out_writer=None):
            act0 = s_act.n
            pe0 = s_pe.n
            src_dve = s_dve.n

            def f(scalar):
                scalar.wait_ge(s_dve.h, src_dve)
                need(scalar, "dts_sp", "dtf", "dts_rows")
                for tt in range(2):
                    for c in range(DM):
                        k = tt * DM + c
                        if k >= 2:
                            scalar.wait_ge(s_pe.h, pe0 + k - 1)
                        s_act.inc(scalar.activation(
                            dts[:, 4 + (k % 2), :], src[c][:, tt * TI:(tt + 1) * TI],
                            AF.Square))
            SC(f)

            def f(tensor):
                tensor.wait_ge(s_pool.h, pool_init)
                tensor.wait_ge(s_dve.h, src_dve)
                need(tensor, "ps0", "ps1", "ps2", "ps3")
                for tt in range(2):
                    for c in range(DM):
                        k = tt * DM + c
                        tensor.matmul(ps[0 + tt][0:1, :], onesb[:, 0:1],
                                      src[c][:, tt * TI:(tt + 1) * TI],
                                      start=(c == 0), stop=(c == DM - 1))
                        tensor.wait_ge(s_act.h, act0 + k + 1)
                        mm = tensor.matmul(ps[2 + tt][0:1, :], ones[:, 0:1],
                                           dts[:, 4 + (k % 2), :],
                                           start=(c == 0), stop=(c == DM - 1))
                        s_pe.inc(mm)
            TE(f)
            pe_sums = s_pe.n

            def f(vector):
                vector.wait_ge(s_pe.h, pe_sums)
                for tt in range(2):
                    sl = slice(tt * TI, (tt + 1) * TI)
                    vector.tensor_scalar(mu_row[:, sl], ps[0 + tt][0:1, :], 1.0 / D_MODEL,
                                         None, op0=ALU.mult)
                    vector.tensor_scalar(m2_row[:, sl], ps[2 + tt][0:1, :], 1.0 / D_MODEL,
                                         None, op0=ALU.mult)
                vector.tensor_tensor(var_row[:], mu_row[:], mu_row[:], op=ALU.mult)
                vector.tensor_tensor(var_row[:], m2_row[:], var_row[:], op=ALU.subtract)
                s_dve.inc(vector.tensor_scalar(var_row[:], var_row[:], 1.0, EPS,
                                               op0=ALU.mult, op1=ALU.add))
            VE(f)
            rel("ps0", s_dve)
            rel("ps1", s_dve)
            rel("ps2", s_dve)
            rel("ps3", s_dve)
            dve_var = s_dve.n

            def f(scalar):
                scalar.wait_ge(s_dve.h, dve_var)
                s_act.inc(scalar.activation(var_row[:], var_row[:], AF.Sqrt))
            SC(f)
            act_sd = s_act.n

            def f(vector):
                vector.wait_ge(s_act.h, act_sd)
                s_dve.inc(vector.reciprocal(rs_row[:], var_row[:]))
            VE(f)
            dve_rs = s_dve.n

            def f(tensor):
                tensor.wait_ge(s_dve.h, dve_rs)
                need(tensor, "ps4", "ps5", "ps6", "ps7")
                for tt in range(2):
                    sl = slice(tt * TI, (tt + 1) * TI)
                    tensor.matmul(ps[4 + tt][:, :], onesr[:], mu_row[:, sl],
                                  start=True, stop=True)
                    mm = tensor.matmul(ps[6 + tt][:, :], onesr[:], rs_row[:, sl],
                                       start=True, stop=True)
                s_pe.inc(mm)
            TE(f)
            rel("dts_rows", s_pe)
            pe_bc = s_pe.n

            def f(scalar):
                scalar.wait_ge(s_pe.h, pe_bc)
                scalar.wait_ge(s_dve.h, src_dve)
                need(scalar, "mubb", "ytt")
                for tt in range(2):
                    scalar.activation(mubb[:, tt, :], ps[4 + tt][:, :], AF.Copy)
                    s_act.inc(scalar.activation(rsbb[:, tt, :], ps[6 + tt][:, :], AF.Copy))
            SC(f)
            rel("ps4", s_act)
            rel("ps5", s_act)
            rel("ps6", s_act)
            rel("ps7", s_act)
            act_bc = s_act.n

            writes = []
            if to_ost:
                for c in range(DM):
                    for tt in range(2):
                        pp = (c * 2 + tt) % 2

                        def f(vector, c=c, tt=tt, pp=pp):
                            if c == 0 and tt == 0:
                                vector.wait_ge(s_act.h, act_bc)
                            sl = slice(tt * TI, (tt + 1) * TI)
                            vector.tensor_tensor(hc_t[:, 0, :], src[c][:, sl],
                                                 mubb[:, tt, :], op=ALU.subtract)
                            need(vector, f"ostw{pp}")
                            s_dve.inc(vector.tensor_tensor(ost[:, pp, :], hc_t[:, 0, :],
                                                           rsbb[:, tt, :], op=ALU.mult))
                        VE(f)
                        writes.append((c, tt, pp, s_dve.n))
                        if out_writer is not None:
                            out_writer(c, tt, pp, s_dve.n)
                rel("mubb", s_dve)
                return s_dve.n, writes

            def f(vector):
                vector.wait_ge(s_act.h, act_bc)
                for c in range(DM):
                    for tt in range(2):
                        sl = slice(tt * TI, (tt + 1) * TI)
                        vector.tensor_tensor(hc_t[:, 0, :], src[c][:, sl], mubb[:, tt, :],
                                             op=ALU.subtract)
                        if apply_w:
                            vector.tensor_tensor(hc_t[:, 1, :], hc_t[:, 0, :],
                                                 rsbb[:, tt, :], op=ALU.mult)
                            mm = vector.tensor_scalar(dst[c][:, sl], hc_t[:, 1, :],
                                                      lnw(l)[:, c:c + 1], lnb(l)[:, c:c + 1],
                                                      op0=ALU.mult, op1=ALU.add)
                        else:
                            mm = vector.tensor_tensor(dst[c][:, sl], hc_t[:, 0, :],
                                                      rsbb[:, tt, :], op=ALU.mult)
                        s_dve.inc(mm)
            VE(f)
            rel("mubb", s_dve)
            return s_dve.n, writes

        out_total = [0]
        for tp in range(n_tp):
            toff0 = tp * TH
            for c in range(DM):
                for tt in range(2):
                    pp = (c * 2 + tt) % 2

                    def f(sync, c=c, tt=tt, pp=pp):
                        need(sync, f"ostr{pp}")
                        s_xd[pp].inc(sync.dma_start(
                            out=ost[:, pp, :],
                            in_=x_hbm[c * P:(c + 1) * P,
                                      toff0 + tt * TI:toff0 + (tt + 1) * TI]), 16)
                    SY(f)
                    xl = s_xd[pp].n

                    def f(vector, c=c, tt=tt, pp=pp, xl=xl):
                        vector.wait_ge(s_xd[pp].h, xl)
                        s_dve.inc(vector.tensor_scalar(
                            x_t[:, c, tt * TI:(tt + 1) * TI], ost[:, pp, :], 1.0, None,
                            op0=ALU.mult))
                    VE(f)
                    rel(f"ostr{pp}", s_dve)
            dbg(f"xload{tp}", x_t[:])

            for l in range(NL):
                xch = [x_t[:, c, :] for c in range(DM)]
                xnch = [xn_t[:, c, :] for c in range(DM)]
                ln_done, _ = layernorm(xch, l, True, xnch)
                dbg(f"xn{tp}_{l}", xn_t[:])

                for ec in range(32):
                    pp = ec % 2

                    def f(sync, l=l, ec=ec):
                        need(sync, "wst")
                        s_dma.inc(sync.dma_start(
                            out=wst[:, 0:1024].rearrange("p (c e) -> p c e", c=DM),
                            in_=in_wT[l][ec].rearrange("(c p) e -> p c e", p=P)), 16)
                    SY(f)
                    wld = s_dma.n

                    def f(vector, wld=wld, pp=pp):
                        vector.wait_ge(s_dma.h, wld)
                        need(vector, f"wbf{pp}")
                        s_dve.inc(vector.tensor_scalar(wbf[:, pp, 0:1024], wst[:, 0:1024],
                                                       1.0, None, op0=ALU.mult))
                    VE(f)
                    rel("wst", s_dve)
                    wcast = s_dve.n

                    def f(tensor, wcast=wcast, ln_done=ln_done, pp=pp):
                        tensor.wait_ge(s_dve.h, max(wcast, ln_done))
                        wv = wbf[:, pp, 0:1024].rearrange("p (c e) -> p c e", c=DM)
                        for tt in range(2):
                            need(tensor, f"ps{2 * pp + tt}")
                            for c in range(DM):
                                mm = tensor.matmul(ps[2 * pp + tt][:, :], wv[:, c, :],
                                                   xn_t[:, c, tt * TI:(tt + 1) * TI],
                                                   start=(c == 0), stop=(c == DM - 1))
                            s_pe.inc(mm)
                    TE(f)
                    rel(f"wbf{pp}", s_pe)
                    mm1 = s_pe.n - 1

                    if ec >= 16:
                        e = ec - 16

                        def f(scalar, mm1=mm1, e=e, pp=pp):
                            if e <= 1:
                                need(scalar, "zs1")
                            if e == 2:
                                need(scalar, "zwbc")
                            for tt in range(2):
                                scalar.wait_ge(s_pe.h, mm1 + tt)
                                s_act.inc(scalar.activation(
                                    z_t[:, e, tt * TI:(tt + 1) * TI],
                                    ps[2 * pp + tt][:, :], AF.Silu))
                        SC(f)
                        rel(f"ps{2 * pp}", s_act)
                        rel(f"ps{2 * pp + 1}", s_act)
                    else:
                        xp = ec % 2

                        def f(scalar, mm1=mm1, xp=xp, pp=pp):
                            need(scalar, f"xin{xp}")
                            for tt in range(2):
                                scalar.wait_ge(s_pe.h, mm1 + tt)
                                s_act.inc(scalar.activation(
                                    xin_c[:, xp, 3 + tt * TI:3 + (tt + 1) * TI],
                                    ps[2 * pp + tt][:, :], AF.Copy))
                        SC(f)
                        rel(f"ps{2 * pp}", s_act)
                        rel(f"ps{2 * pp + 1}", s_act)
                        evac = s_act.n

                        def f(vector, l=l, ec=ec, xp=xp, evac=evac):
                            vector.wait_ge(s_act.h, evac)
                            vector.tensor_scalar(xin_c[:, xp, 0:3], halo[:, l, ec, :],
                                                 1.0, None, op0=ALU.mult)
                            vector.tensor_scalar(halo[:, l, ec, :],
                                                 xin_c[:, xp, TH:TH + 3], 1.0, None,
                                                 op0=ALU.mult)
                            need(vector, "diag")
                            for k in range(4):
                                s_dve.inc(vector.tensor_scalar(
                                    diag[:, k, :], idn[:], cwv(l)[:, ec, k:k + 1],
                                    None, op0=ALU.mult))
                        VE(f)
                        dve_cd = s_dve.n

                        def f(tensor, dve_cd=dve_cd, xp=xp):
                            tensor.wait_ge(s_dve.h, dve_cd)
                            for tt in range(2):
                                need(tensor, f"ps{4 + 2 * xp + tt}")
                                for k in range(4):
                                    mm = tensor.matmul(
                                        ps[4 + 2 * xp + tt][:, :], diag[:, k, :],
                                        xin_c[:, xp, k + tt * TI:k + tt * TI + TI],
                                        start=(k == 0), stop=(k == 3))
                                s_pe.inc(mm)
                        TE(f)
                        rel("diag", s_pe)
                        rel(f"xin{xp}", s_pe, s_dve)
                        cv1 = s_pe.n - 1

                        def f(scalar, cv1=cv1, ec=ec, l=l, xp=xp):
                            need(scalar, f"gfull{ec}")
                            for tt in range(2):
                                scalar.wait_ge(s_pe.h, cv1 + tt)
                                s_act.inc(scalar.activation(
                                    xc_t[:, ec, tt * TI:(tt + 1) * TI],
                                    ps[4 + 2 * xp + tt][:, :], AF.Silu,
                                    bias=cbv(l)[:, ec:ec + 1]))
                        SC(f)
                        rel(f"ps{4 + 2 * xp}", s_act)
                        rel(f"ps{4 + 2 * xp + 1}", s_act)
                xc_all = s_act.n
                dbg(f"xc{tp}_{l}", xc_t[:])
                dbg(f"z{tp}_{l}", z_t[:])

                xpc = 0
                for hf in range(2):
                    def f(sync, l=l, hf=hf):
                        need(sync, "wst")
                        s_dma.inc(sync.dma_start(
                            out=wst[:, 0:768].rearrange("p (c r) -> p c r", c=8),
                            in_=xp_wT[l][hf * 8:(hf + 1) * 8].rearrange("c p r -> p c r")), 16)
                    SY(f)
                    xpld = s_dma.n

                    def f(vector, xpld=xpld, hf=hf):
                        vector.wait_ge(s_dma.h, xpld)
                        if hf == 0:
                            need(vector, "wbf0")
                        s_dve.inc(vector.tensor_scalar(
                            wbf[:, 0, hf * 768:(hf + 1) * 768], wst[:, 0:768], 1.0,
                            None, op0=ALU.mult))
                    VE(f)
                    rel("wst", s_dve)
                    xpc = s_dve.n

                def f(tensor, xpc=xpc, xc_all=xc_all):
                    tensor.wait_ge(s_dve.h, xpc)
                    tensor.wait_ge(s_act.h, xc_all)
                    wv = wbf[:, 0, 0:EC * 96].rearrange("p (c r) -> p c r", c=EC)
                    for tt in range(2):
                        sl = slice(tt * TI, (tt + 1) * TI)
                        need(tensor, f"ps{0 + tt}", f"ps{2 + tt}", f"ps{4 + tt}")
                        for c in range(EC):
                            tensor.matmul(ps[0 + tt][0:R, :], wv[:, c, 0:R],
                                          xc_t[:, c, sl], start=(c == 0),
                                          stop=(c == EC - 1))
                        for c in range(EC):
                            tensor.matmul(ps[2 + tt][0:NST, :], wv[:, c, R:R + NST],
                                          xc_t[:, c, sl], start=(c == 0),
                                          stop=(c == EC - 1))
                        for c in range(EC):
                            mm = tensor.matmul(ps[4 + tt][0:NST, :], wv[:, c, R + NST:96],
                                               xc_t[:, c, sl], start=(c == 0),
                                               stop=(c == EC - 1))
                        s_pe.inc(mm)
                TE(f)
                rel("wbf0", s_pe)
                dbl1 = s_pe.n - 1

                def f(scalar, dbl1=dbl1):
                    need(scalar, "bcrows")
                    for tt in range(2):
                        sl = slice(tt * TI, (tt + 1) * TI)
                        scalar.wait_ge(s_pe.h, dbl1 + tt)
                        scalar.activation(dtrb[:, sl], ps[0 + tt][0:R, :], AF.Copy)
                        scalar.activation(brow[:, sl], ps[2 + tt][0:NST, :], AF.Copy)
                        s_act.inc(scalar.activation(crow[:, sl], ps[4 + tt][0:NST, :],
                                                    AF.Copy))
                SC(f)
                for i in range(6):
                    rel(f"ps{i}", s_act)
                bc_ev = s_act.n
                dbg(f"dtr{tp}_{l}", dtrb[:])
                dbg(f"brow{tp}_{l}", brow[:])

                dtwc = 0
                for hf in range(2):
                    def f(sync, l=l, hf=hf):
                        need(sync, "wst")
                        s_dma.inc(sync.dma_start(
                            out=wst[0:R, 0:1024].rearrange("p (c e) -> p c e", c=8),
                            in_=dt_wT[l][hf * 8:(hf + 1) * 8].rearrange("c p e -> p c e")), 16)
                    SY(f)
                    dtld = s_dma.n

                    def f(vector, dtld=dtld, hf=hf):
                        vector.wait_ge(s_dma.h, dtld)
                        if hf == 0:
                            need(vector, "dtwb")
                        s_dve.inc(vector.tensor_scalar(
                            dtwb[:, hf * 1024:(hf + 1) * 1024], wst[0:R, 0:1024], 1.0,
                            None, op0=ALU.mult))
                    VE(f)
                    rel("wst", s_dve)
                    dtwc = s_dve.n
                dtw_v = dtwb[:].rearrange("p (c e) -> p c e", c=EC)

                for ti in range(2):
                    toff = ti * TI
                    for nh in range(2):
                        ract0 = s_act.n

                        def f(tensor, ti=ti, nh=nh, bc_ev=bc_ev, ract0=ract0):
                            tensor.wait_ge(s_act.h, bc_ev)
                            tensor.wait_ge(s_dve.h, init_done)
                            for j in range(8):
                                n = nh * 8 + j
                                bb_, cb_ = (0, 1) if j % 2 == 0 else (2, 3)
                                need(tensor, f"ps{bb_}", f"ps{cb_}")
                                if j >= 2:
                                    tensor.wait_ge(s_act.h, ract0 + j - 1)
                                tensor.matmul(ps[bb_][:, :], sel[:, n, :],
                                              brow[:, toff:toff + TI], start=True, stop=True)
                                mm = tensor.matmul(ps[cb_][:, :], sel[:, n, :],
                                                   crow[:, toff:toff + TI], start=True,
                                                   stop=True)
                                s_pe.inc(mm)
                        TE(f)
                        rep_mm = s_pe.n

                        def f(scalar, rep_mm=rep_mm):
                            need(scalar, "brep")
                            for j in range(8):
                                scalar.wait_ge(s_pe.h, rep_mm - 8 + j + 1)
                                bb_, cb_ = (0, 1) if j % 2 == 0 else (2, 3)
                                scalar.activation(brep[:, j, :], ps[bb_][:, :], AF.Copy)
                                s_act.inc(scalar.activation(crep[:, j, :], ps[cb_][:, :],
                                                            AF.Copy))
                        SC(f)
                        for i in range(4):
                            rel(f"ps{i}", s_act)
                        rep_ev = s_act.n
                        if l == 0 and ti == 0 and nh == 0:
                            dbg("brep00", brep[:])
                            dbg("crep00", crep[:])
                        if l == 0 and ti == 1 and nh == 0:
                            dbg("brep10", brep[:])
                            dbg("crep10", crep[:])

                        for dc in range(EC):
                            def f(tensor, dc=dc, ti=ti, dtwc=dtwc):
                                tensor.wait_ge(s_dve.h, dtwc)
                                need(tensor, "ps6")
                                s_pe.inc(tensor.matmul(ps[6][:, :], dtw_v[:, dc, :],
                                                       dtrb[:, toff:toff + TI],
                                                       start=True, stop=True))
                            TE(f)
                            dtmm = s_pe.n

                            def f(scalar, dtmm=dtmm, l=l, dc=dc):
                                scalar.wait_ge(s_pe.h, dtmm)
                                need(scalar, "dts_sp")
                                scalar.activation(dts[:, 1, :], ps[6][:, :], AF.Abs,
                                                  bias=dtb(l)[:, dc:dc + 1])
                                scalar.activation(dts[:, 2, :], dts[:, 1, :], AF.Exp,
                                                  scale=-1.0)
                                scalar.activation(dts[:, 3, :], dts[:, 2, :], AF.Ln,
                                                  bias=1.0)
                                s_act.inc(scalar.activation(dts[:, 4, :], ps[6][:, :],
                                                            AF.Relu,
                                                            bias=dtb(l)[:, dc:dc + 1]))
                            SC(f)
                            rel("ps6", s_act)
                            spd = s_act.n

                            def f(vector, spd=spd, dc=dc, ti=ti):
                                vector.wait_ge(s_act.h, spd)
                                need(vector, "dtf")
                                vector.tensor_tensor(dts[:, 5, :], dts[:, 3, :],
                                                     dts[:, 4, :], op=ALU.add)
                                need(vector, "ytt")
                                vector.tensor_scalar(yt_t[:, 0, :], dts[:, 5, :], 1.0,
                                                     None, op0=ALU.mult)
                                s_dve.inc(vector.tensor_tensor(
                                    yt_t[:, 1, :], yt_t[:, 0, :],
                                    xc_t[:, dc, toff:toff + TI], op=ALU.mult))
                            VE(f)
                            rel("dts_sp", s_dve)
                            dtrdy = s_dve.n
                            dtu_v = yt_t[:, 1, :]
                            if l == 0 and ti == 0 and nh == 0 and dc == 0:
                                dbg("dt000", dts[:, 5, :])
                                dbg("dtu000", yt_t[:, 1, :])
                            if l == 0 and ti == 1 and nh == 0 and dc == 0:
                                dbg("dt100", dts[:, 5, :])
                                dbg("dtu100", yt_t[:, 1, :])

                            for g in range(NGH):
                                qp = g % 2

                                def f(scalar, g=g, qp=qp, dc=dc, l=l, nh=nh, dtrdy=dtrdy):
                                    scalar.wait_ge(s_dve.h, dtrdy)
                                    need(scalar, f"dA{qp}")
                                    for j in range(GN):
                                        n = nh * 8 + g * GN + j
                                        s_act.inc(scalar.activation(
                                            dA_t[:, qp, j, :], dts[:, 5, :], AF.Exp,
                                            scale=A_v(l)[:, dc, n:n + 1]))
                                SC(f)
                                dAd = s_act.n

                                def f(vector, g=g, qp=qp, dc=dc, l=l, nh=nh, ti=ti,
                                      dAd=dAd, rep_ev=rep_ev):
                                    vector.wait_ge(s_act.h, max(dAd, rep_ev))
                                    j0 = g * GN
                                    s_dve.inc(vector.tensor_tensor(
                                        b_t[:],
                                        dtu_v.unsqueeze(1).broadcast_to([P, GN, TI]),
                                        brep[:, j0:j0 + GN, :], op=ALU.mult))
                                    vector.wait_ge(s_dve.h, s_dve.n)
                                    for j in range(GN):
                                        n = nh * 8 + g * GN + j
                                        mm = vector.tensor_tensor_scan(
                                            h_t[:, j, :], dA_t[:, qp, j, :], b_t[:, j, :],
                                            carry[:, l, dc, n:n + 1],
                                            op0=ALU.mult, op1=ALU.add)
                                    s_dve.inc(mm)
                                    vector.wait_ge(s_dve.h, s_dve.n)
                                    vector.tensor_scalar(
                                        carry[:, l, dc, nh * 8 + j0:nh * 8 + j0 + GN],
                                        h_t[:, :, TI - 1], 1.0, None, op0=ALU.mult)
                                    vector.tensor_tensor(hc_t[:], h_t[:],
                                                         crep[:, j0:j0 + GN, :],
                                                         op=ALU.mult)
                                    if g == 0:
                                        mm = vector.tensor_tensor(ya_t[:, 0, :],
                                                                  hc_t[:, 0, :],
                                                                  hc_t[:, 1, :], op=ALU.add)
                                    else:
                                        vector.tensor_tensor(ya_t[:, 1, :], hc_t[:, 0, :],
                                                             hc_t[:, 1, :], op=ALU.add)
                                        mm = vector.tensor_tensor(ya_t[:, 0, :],
                                                                  ya_t[:, 0, :],
                                                                  ya_t[:, 1, :], op=ALU.add)
                                    s_dve.inc(mm)
                                VE(f)
                                rel(f"dA{qp}", s_dve)

                            if l == 0 and ti == 0 and nh == 0 and dc == 0:
                                dbg("h000", h_t[:])
                                dbg("dA000", dA_t[:])
                            if l == 0 and ti == 1 and nh == 0 and dc == 0:
                                dbg("h100", h_t[:])
                                dbg("dA100", dA_t[:])

                            def f(vector, dc=dc, ti=ti, nh=nh, l=l):
                                if nh == 0:
                                    need(vector, "y_g")
                                    mm = vector.tensor_scalar(y_g[:, dc, :], ya_t[:, 0, :],
                                                              1.0, None, op0=ALU.mult)
                                else:
                                    vector.tensor_tensor(ya_t[:, 1, :], y_g[:, dc, :],
                                                         ya_t[:, 0, :], op=ALU.add)
                                    vector.tensor_scalar(yt_t[:, 0, :],
                                                         xc_t[:, dc, toff:toff + TI],
                                                         dpv(l)[:, dc:dc + 1], None,
                                                         op0=ALU.mult)
                                    vector.tensor_tensor(yt_t[:, 0, :], ya_t[:, 1, :],
                                                         yt_t[:, 0, :], op=ALU.add)
                                    mm = vector.tensor_tensor(
                                        y_g[:, dc, :], yt_t[:, 0, :],
                                        z_t[:, dc, toff:toff + TI], op=ALU.mult)
                                s_dve.inc(mm)
                            VE(f)
                            rel("ytt", s_dve)
                            rel("dtf", s_dve)
                            if l == 0 and ti == 0 and nh == 0 and dc == 0:
                                dbg("ya000", y_g[:, 0, :])
                        rel("brep", s_dve)
                    yg_done = s_dve.n
                    dbg(f"yg{tp}_{l}_{ti}", y_g[:])

                    for dm_c in range(DM):
                        pp = dm_c % 2

                        owc = 0
                        for hf in range(2):
                            def f(sync, l=l, dm_c=dm_c, hf=hf):
                                need(sync, "wst")
                                s_dma.inc(sync.dma_start(
                                    out=wst[:].rearrange("p (c e) -> p c e", c=DM),
                                    in_=out_wT[l][dm_c][hf * 1024:(hf + 1) * 1024]
                                    .rearrange("(c p) e -> p c e", p=P)), 16)
                            SY(f)
                            owld = s_dma.n

                            def f(vector, owld=owld, pp=pp, hf=hf):
                                vector.wait_ge(s_dma.h, owld)
                                if hf == 0:
                                    need(vector, f"wbf{pp}")
                                s_dve.inc(vector.tensor_scalar(
                                    wbf[:, pp, hf * 1024:(hf + 1) * 1024], wst[:], 1.0,
                                    None, op0=ALU.mult))
                            VE(f)
                            rel("wst", s_dve)
                            owc = s_dve.n

                        def f(tensor, owc=owc, yg_done=yg_done, pp=pp):
                            tensor.wait_ge(s_dve.h, max(owc, yg_done))
                            need(tensor, f"ps{6 + pp}")
                            wv = wbf[:, pp, :].rearrange("p (c e) -> p c e", c=EC)
                            for c in range(EC):
                                mm = tensor.matmul(ps[6 + pp][:, :], wv[:, c, :],
                                                   y_g[:, c, :], start=(c == 0),
                                                   stop=(c == EC - 1))
                            s_pe.inc(mm)
                        TE(f)
                        rel(f"wbf{pp}", s_pe)
                        opmm = s_pe.n

                        def f(scalar, opmm=opmm, pp=pp):
                            scalar.wait_ge(s_pe.h, opmm)
                            need(scalar, "ytt")
                            s_act.inc(scalar.activation(yt_t[:, pp, :], ps[6 + pp][:, :],
                                                        AF.Copy))
                        SC(f)
                        rel(f"ps{6 + pp}", s_act)
                        opev = s_act.n

                        def f(vector, opev=opev, dm_c=dm_c, ti=ti, pp=pp):
                            vector.wait_ge(s_act.h, opev)
                            s_dve.inc(vector.tensor_tensor(
                                x_t[:, dm_c, toff:toff + TI], yt_t[:, pp, :],
                                x_t[:, dm_c, toff:toff + TI], op=ALU.add))
                        VE(f)
                        rel("ytt", s_dve)
                    rel("y_g", s_pe)
                rel("dtwb", s_pe)
                rel("bcrows", s_pe)
                dbg(f"xl{tp}_{l}", x_t[:])

            for c in range(DM):
                for tt in range(2):
                    pp = (c * 2 + tt) % 2

                    def f(sync, c=c, tt=tt, pp=pp):
                        need(sync, f"ostr{pp}")
                        s_xd[pp].inc(sync.dma_start(
                            out=ost[:, pp, :],
                            in_=x_hbm[c * P:(c + 1) * P,
                                      toff0 + tt * TI:toff0 + (tt + 1) * TI]), 16)
                    SY(f)
                    xl = s_xd[pp].n

                    def f(vector, c=c, tt=tt, pp=pp, xl=xl):
                        vector.wait_ge(s_xd[pp].h, xl)
                        need(vector, "ytt")
                        vector.tensor_scalar(yt_t[:, pp, :], ost[:, pp, :], 1.0, None,
                                             op0=ALU.mult)
                        s_dve.inc(vector.tensor_tensor(
                            x_t[:, c, tt * TI:(tt + 1) * TI], yt_t[:, pp, :],
                            x_t[:, c, tt * TI:(tt + 1) * TI], op=ALU.add))
                    VE(f)
                    rel(f"ostr{pp}", s_dve)
                    rel("ytt", s_dve)

            xch = [x_t[:, c, :] for c in range(DM)]
            ln1, _ = layernorm(xch, 0, False, xch)
            dbg(f"pl1_{tp}", x_t[:])

            def f(sync):
                need(sync, "wst")
                s_dma.inc(sync.dma_start(
                    out=wst[:, 0:DM * 8].rearrange("p (c e) -> p c e", c=DM),
                    in_=wgT.rearrange("(c p) e -> p c e", p=P)), 16)
            SY(f)
            wgld = s_dma.n

            def f(vector, wgld=wgld):
                vector.wait_ge(s_dma.h, wgld)
                need(vector, "wbf0")
                s_dve.inc(vector.tensor_scalar(wbf[:, 0, 0:DM * 8], wst[:, 0:DM * 8],
                                               1.0, None, op0=ALU.mult))
            VE(f)
            rel("wst", s_dve)
            wgc = s_dve.n

            def f(tensor, wgc=wgc, ln1=ln1):
                tensor.wait_ge(s_dve.h, max(wgc, ln1))
                wv = wbf[:, 0, 0:DM * 8].rearrange("p (c e) -> p c e", c=DM)
                for tt in range(2):
                    need(tensor, f"ps{0 + tt}")
                    for c in range(DM):
                        mm = tensor.matmul(ps[0 + tt][0:8, :], wv[:, c, :],
                                           x_t[:, c, tt * TI:(tt + 1) * TI],
                                           start=(c == 0), stop=(c == DM - 1))
                    s_pe.inc(mm)
            TE(f)
            rel("wbf0", s_pe)
            gmm = s_pe.n

            def f(scalar, gmm=gmm):
                need(scalar, "dts_sp", "dtf", "dts_rows")
                for tt in range(2):
                    scalar.wait_ge(s_pe.h, gmm - 1 + tt)
                    s_act.inc(scalar.activation(dts_flat[0:8, tt * TI:(tt + 1) * TI],
                                                ps[0 + tt][0:8, :], AF.Copy))
            SC(f)
            rel("ps0", s_act)
            rel("ps1", s_act)
            gev = s_act.n
            gate_fm = dts_flat[0:8, 0:TH]
            dbg(f"gfm{tp}", dts_flat[0:8, 0:TH])


            tact0 = s_act.n

            def f(tensor, gev=gev, tact0=tact0):
                tensor.wait_ge(s_act.h, gev)
                for tc in range(DM):
                    need(tensor, f"ps{2 + tc % 2}")
                    if tc >= 2:
                        tensor.wait_ge(s_act.h, tact0 + tc - 1)
                    s_pe.inc(tensor.transpose((ps[2] if tc % 2 == 0 else ps[3])[:, 0:8],
                                              gate_fm[:, tc * P:(tc + 1) * P],
                                              idn[0:8, 0:8]))
            TE(f)
            rel("dts_rows", s_pe)
            gtmm = s_pe.n

            def f(scalar, gtmm=gtmm):
                need(scalar, "gt")
                for tc in range(DM):
                    scalar.wait_ge(s_pe.h, gtmm - DM + tc + 1)
                    s_act.inc(scalar.activation(
                        gt_t[:, tc, 0:8], (ps[2] if tc % 2 == 0 else ps[3])[:, 0:8],
                        AF.Copy))
            SC(f)
            rel("ps2", s_act)
            rel("ps3", s_act)
            gtev = s_act.n
            dbg(f"gtok{tp}", gt_t[:])

            def f(vector, gtev=gtev):
                vector.wait_ge(s_act.h, gtev)

                def step(instr):
                    s_dve.inc(instr)
                    vector.wait_ge(s_dve.h, s_dve.n)
                step(vector.tensor_copy(gg[:], gt_t[:, :, 0:8].transpose([0, 2, 1])))
                step(vector.tensor_tensor(rt_t[:], gg[:, 0:4, :], gg[:, 4:8, :],
                                          op=ALU.max))
                step(vector.tensor_tensor(ggm[:, 2:4, :], rt_t[:, 0:2, :],
                                          rt_t[:, 2:4, :], op=ALU.max))
                step(vector.tensor_tensor(ggm[:, 0:1, :], ggm[:, 2:3, :],
                                          ggm[:, 3:4, :], op=ALU.max))
                step(vector.tensor_tensor(wtok[:], gg[:],
                                          ggm[:, 0:1, :].broadcast_to([P, 8, 8]),
                                          op=ALU.is_equal))
                step(vector.tensor_scalar(rt_t[:], wtok[:, 0:4, :], -1e30, None,
                                          op0=ALU.mult))
                step(vector.tensor_tensor(rt_t[:], gg[:, 0:4, :], rt_t[:], op=ALU.add))
                step(vector.tensor_scalar(ggm[:, 2:4, :], wtok[:, 4:6, :], -1e30, None,
                                          op0=ALU.mult))
                step(vector.tensor_tensor(ggm[:, 2:4, :], gg[:, 4:6, :], ggm[:, 2:4, :],
                                          op=ALU.add))
                step(vector.tensor_scalar(gg[:, 0:2, :], wtok[:, 6:8, :], -1e30, None,
                                          op0=ALU.mult))
                step(vector.tensor_tensor(gg[:, 0:2, :], gg[:, 6:8, :], gg[:, 0:2, :],
                                          op=ALU.add))
                step(vector.tensor_tensor(rt_t[:, 0:2, :], rt_t[:, 0:2, :],
                                          rt_t[:, 2:4, :], op=ALU.max))
                step(vector.tensor_tensor(rt_t[:, 0:2, :], rt_t[:, 0:2, :],
                                          ggm[:, 2:4, :], op=ALU.max))
                step(vector.tensor_tensor(rt_t[:, 0:2, :], rt_t[:, 0:2, :],
                                          gg[:, 0:2, :], op=ALU.max))
                step(vector.tensor_tensor(ggm[:, 1:2, :], rt_t[:, 0:1, :],
                                          rt_t[:, 1:2, :], op=ALU.max))
                step(vector.tensor_tensor(ggm[:, 2:3, :], ggm[:, 0:1, :],
                                          ggm[:, 1:2, :], op=ALU.subtract))
            VE(f)
            dfd = s_dve.n

            def f(scalar, dfd=dfd):
                scalar.wait_ge(s_dve.h, dfd)
                s_act.inc(scalar.activation(ggm[:, 3, :], ggm[:, 2, :], AF.Sigmoid))
            SC(f)
            p1d = s_act.n

            def f(vector, p1d=p1d):
                vector.wait_ge(s_act.h, p1d)
                need(vector, "gg2")

                def step(instr):
                    s_dve.inc(instr)
                    vector.wait_ge(s_dve.h, s_dve.n)
                step(vector.tensor_copy(gg[:], gt_t[:, :, 0:8].transpose([0, 2, 1])))
                step(vector.tensor_tensor(wtok[:], gg[:],
                                          ggm[:, 0:1, :].broadcast_to([P, 8, 8]),
                                          op=ALU.is_equal))
                step(vector.tensor_tensor(gg[:], gg[:],
                                          ggm[:, 1:2, :].broadcast_to([P, 8, 8]),
                                          op=ALU.is_equal))
                p1b = ggm[:, 3:4, :].broadcast_to([P, 8, 8])
                step(vector.tensor_tensor(wtok[:], wtok[:], gg[:], op=ALU.subtract))
                step(vector.tensor_tensor(wtok[:], wtok[:], p1b, op=ALU.mult))
                step(vector.tensor_tensor(wtok[:], wtok[:], gg[:], op=ALU.add))
                step(vector.tensor_copy(gt_t[:, :, 0:8], wtok[:].transpose([0, 2, 1])))
                step(vector.tensor_copy(wtok[:], gt_t[:, :, 0:8]))
            VE(f)
            rel("gt", s_dve)
            rel("gg2", s_dve)
            wtd = s_dve.n
            dbg(f"wtok{tp}", wtok[:])
            dbg(f"gtok2_{tp}", gt_t[:])
            dbg(f"p1_{tp}", rt_t[:])

            wact0 = s_act.n

            def f(tensor, wtd=wtd, wact0=wact0):
                tensor.wait_ge(s_dve.h, wtd)
                for tc in range(DM):
                    need(tensor, f"ps{2 + tc % 2}")
                    if tc >= 2:
                        tensor.wait_ge(s_act.h, wact0 + tc - 1)
                    s_pe.inc(tensor.transpose((ps[2] if tc % 2 == 0 else ps[3])[0:8, 0:P],
                                              wtok[:, tc, :], idn[:]))
            TE(f)
            wfmm = s_pe.n

            def f(scalar, wfmm=wfmm):
                need(scalar, "wfm")
                for tc in range(DM):
                    scalar.wait_ge(s_pe.h, wfmm - DM + tc + 1)
                    s_act.inc(scalar.activation(wfm[:, tc * P:(tc + 1) * P],
                                                (ps[2] if tc % 2 == 0 else ps[3])[0:8, 0:P],
                                                AF.Copy))
            SC(f)
            rel("ps2", s_act)
            rel("ps3", s_act)
            wfmd = s_act.n

            g_full = xc_t
            y_moe = xn_t
            for e in range(NE):
                def f(tensor, e=e, wfmd=wfmd):
                    tensor.wait_ge(s_act.h, wfmd)
                    for tt in range(2):
                        need(tensor, f"ps{2 + tt}")
                        s_pe.inc(tensor.matmul(ps[2 + tt][:, :], sel[0:8, e, :],
                                               wfm[:, tt * TI:(tt + 1) * TI],
                                               start=True, stop=True))
                TE(f)
                wbmm = s_pe.n

                def f(scalar, wbmm=wbmm):
                    need(scalar, "zwbc")
                    for tt in range(2):
                        scalar.wait_ge(s_pe.h, wbmm - 1 + tt)
                        s_act.inc(scalar.activation(z_t[:, 2, tt * TI:(tt + 1) * TI],
                                                    ps[2 + tt][:, :], AF.Copy))
                SC(f)
                rel("ps2", s_act)
                rel("ps3", s_act)
                wbcd = s_act.n

                for hc in range(16):
                    pp = hc % 2

                    wc = 0
                    for wi, wt in ((0, w1T), (1, w2T)):
                        def f(sync, e=e, hc=hc, wt=wt):
                            need(sync, "wst")
                            s_dma.inc(sync.dma_start(
                                out=wst[:].rearrange("p (c x) -> p c x", c=DM),
                                in_=wt[e][hc].rearrange("(c p) x -> p c x", p=P)), 16)
                        SY(f)
                        wld = s_dma.n

                        def f(vector, wld=wld, pp=pp, wi=wi):
                            vector.wait_ge(s_dma.h, wld)
                            if wi == 0:
                                need(vector, f"wbf{pp}")
                            s_dve.inc(vector.tensor_scalar(
                                wbf[:, pp, wi * 1024:(wi + 1) * 1024], wst[:], 1.0,
                                None, op0=ALU.mult))
                        VE(f)
                        rel("wst", s_dve)
                        wc = s_dve.n

                    def f(tensor, wc=wc, pp=pp, ln1=ln1):
                        tensor.wait_ge(s_dve.h, max(wc, ln1))
                        w1v = wbf[:, pp, 0:1024].rearrange("p (c x) -> p c x", c=DM)
                        w2v = wbf[:, pp, 1024:2048].rearrange("p (c x) -> p c x", c=DM)
                        for tt in range(2):
                            sl = slice(tt * TI, (tt + 1) * TI)
                            need(tensor, f"ps{0 + tt}")
                            for c in range(DM):
                                tensor.matmul(ps[0 + tt][:, :], w1v[:, c, :], x_t[:, c, sl],
                                              start=(c == 0), stop=(c == DM - 1))
                            need(tensor, f"ps{4 + tt}")
                            for c in range(DM):
                                mm = tensor.matmul(ps[4 + tt][:, :], w2v[:, c, :],
                                                   x_t[:, c, sl],
                                                   start=(c == 0), stop=(c == DM - 1))
                            s_pe.inc(mm)
                    TE(f)
                    rel(f"wbf{pp}", s_pe)
                    hm1 = s_pe.n - 1

                    def f(scalar, hm1=hm1):
                        need(scalar, "zs1")
                        for tt in range(2):
                            scalar.wait_ge(s_pe.h, hm1 + tt)
                            scalar.activation(z_t[:, 0, tt * TI:(tt + 1) * TI],
                                              (ps[0] if tt == 0 else ps[1])[:, :], AF.Silu)
                            s_act.inc(scalar.activation(
                                z_t[:, 1, tt * TI:(tt + 1) * TI],
                                (ps[4] if tt == 0 else ps[5])[:, :], AF.Copy))
                    SC(f)
                    rel("ps0", s_act)
                    rel("ps1", s_act)
                    rel("ps4", s_act)
                    rel("ps5", s_act)
                    hev = s_act.n

                    def f(vector, hev=hev, hc=hc):
                        vector.wait_ge(s_act.h, hev)
                        need(vector, f"gfull{hc}")
                        s_dve.inc(vector.tensor_tensor(g_full[:, hc, :], z_t[:, 0, :],
                                                       z_t[:, 1, :], op=ALU.mult))
                    VE(f)
                    rel("zs1", s_dve)
                    gd = s_dve.n

                    def f(gpsimd, gd=gd, hc=hc, wbcd=wbcd):
                        gpsimd.wait_ge(s_dve.h, gd)
                        gpsimd.wait_ge(s_act.h, wbcd)
                        s_pool.inc(gpsimd.tensor_tensor(g_full[:, hc, :], g_full[:, hc, :],
                                                        z_t[:, 2, :], op=ALU.mult))
                    GP(f)
                rel("zwbc", s_pool)
                gw_done = s_pool.n

                for dm_c in range(DM):
                    pp = dm_c % 2

                    w3c = 0
                    for hf in range(2):
                        def f(sync, e=e, dm_c=dm_c, hf=hf):
                            need(sync, "wst")
                            s_dma.inc(sync.dma_start(
                                out=wst[:].rearrange("p (c x) -> p c x", c=DM),
                                in_=w3T[e][dm_c][hf * 1024:(hf + 1) * 1024]
                                .rearrange("(c p) x -> p c x", p=P)), 16)
                        SY(f)
                        w3ld = s_dma.n

                        def f(vector, w3ld=w3ld, pp=pp, hf=hf):
                            vector.wait_ge(s_dma.h, w3ld)
                            if hf == 0:
                                need(vector, f"wbf{pp}")
                            s_dve.inc(vector.tensor_scalar(
                                wbf[:, pp, hf * 1024:(hf + 1) * 1024], wst[:], 1.0,
                                None, op0=ALU.mult))
                        VE(f)
                        rel("wst", s_dve)
                        w3c = s_dve.n

                    def f(tensor, w3c=w3c, gw_done=gw_done, pp=pp):
                        tensor.wait_ge(s_dve.h, w3c)
                        tensor.wait_ge(s_pool.h, gw_done)
                        wv = wbf[:, pp, :].rearrange("p (c x) -> p c x", c=EC)
                        for tt in range(2):
                            need(tensor, f"ps{(0 if pp == 0 else 4) + tt}")
                            for c in range(EC):
                                mm = tensor.matmul(
                                    ps[(0 if pp == 0 else 4) + tt][:, :], wv[:, c, :],
                                    g_full[:, c, tt * TI:(tt + 1) * TI],
                                    start=(c == 0), stop=(c == EC - 1))
                            s_pe.inc(mm)
                    TE(f)
                    rel(f"wbf{pp}", s_pe)
                    y1 = s_pe.n - 1

                    def f(scalar, y1=y1, pp=pp):
                        need(scalar, "ytt")
                        for tt in range(2):
                            scalar.wait_ge(s_pe.h, y1 + tt)
                            s_act.inc(scalar.activation(
                                yt_t[:, tt, :],
                                (ps[0 + tt] if pp == 0 else ps[4 + tt])[:, :], AF.Copy))
                    SC(f)
                    rel(f"ps{0 if pp == 0 else 4}", s_act)
                    rel(f"ps{1 if pp == 0 else 5}", s_act)
                    yev = s_act.n

                    def f(vector, yev=yev, dm_c=dm_c, e=e, hc_last=(e == NE - 1)):
                        vector.wait_ge(s_act.h, yev)
                        for tt in range(2):
                            sl = slice(tt * TI, (tt + 1) * TI)
                            if e == 0:
                                mm = vector.tensor_scalar(y_moe[:, dm_c, sl],
                                                          yt_t[:, tt, :], 1.0, None,
                                                          op0=ALU.mult)
                            else:
                                mm = vector.tensor_tensor(y_moe[:, dm_c, sl],
                                                          y_moe[:, dm_c, sl],
                                                          yt_t[:, tt, :], op=ALU.add)
                            s_dve.inc(mm)
                    VE(f)
                    rel("ytt", s_dve)
                for hc in range(16):
                    rel(f"gfull{hc}", s_pe)
            rel("wfm", s_pe)
            moe_done = s_dve.n

            def f(vector, moe_done=moe_done):
                for c in range(DM):
                    s_dve.inc(vector.tensor_tensor(x_t[:, c, :], x_t[:, c, :],
                                                   y_moe[:, c, :], op=ALU.add))
            VE(f)
            dbg(f"moe{tp}", x_t[:])

            def out_writer(c, tt, pp, dval, tp=tp):
                def f(gpsimd, c=c, tt=tt, pp=pp, dval=dval, tp=tp):
                    gpsimd.wait_ge(s_dve.h, dval)
                    s_ou[pp].inc(gpsimd.dma_start(
                        out=y_hbm[c * P:(c + 1) * P,
                                  tp * TH + tt * TI:tp * TH + (tt + 1) * TI],
                        in_=ost[:, pp, :]), 16)
                GP(f)
                free[f"ostw{pp}"] = [(s_ou[pp], s_ou[pp].n)]
                free[f"ostr{pp}"] = [(s_ou[pp], s_ou[pp].n)]

            xch = [x_t[:, c, :] for c in range(DM)]
            layernorm(xch, 0, False, xch, to_ost=True, out_writer=out_writer)

        def f(gpsimd):
            gpsimd.wait_ge(s_ou[0].h, s_ou[0].n)
            gpsimd.wait_ge(s_ou[1].h, s_ou[1].n)
            gpsimd.wait_ge(s_tap.h, s_tap.n)
        GP(f)

    return nc, sorted(tap_seen)



def _prep_weights(inp):
    out = {}
    for l in range(NL):
        iwT = np.ascontiguousarray(inp["L0_attn_in_proj_w"][l].astype(np.float32).T)
        out[f"in_wT{l}"] = np.ascontiguousarray(
            iwT.reshape(D_MODEL, 32, 128).transpose(1, 0, 2))
        owT = np.ascontiguousarray(inp["L0_attn_out_proj_w"][l].astype(np.float32).T)
        out[f"out_wT{l}"] = np.ascontiguousarray(
            owT.reshape(D_INNER, DM, 128).transpose(1, 0, 2))
        xpT = np.ascontiguousarray(inp["L0_attn_x_proj_w"][l].astype(np.float32).T)
        out[f"xp_wT{l}"] = np.ascontiguousarray(xpT.reshape(EC, 128, 96))
        dwT = np.ascontiguousarray(inp["L0_attn_dt_proj_w"][l].astype(np.float32).T)
        out[f"dt_wT{l}"] = np.ascontiguousarray(
            dwT.reshape(R, EC, 128).transpose(1, 0, 2))
        sp = np.zeros((P, 384), np.float32)
        sp[:, 0:8] = inp["L0_attn_ln_w"][l].astype(np.float32).reshape(DM, P).T
        sp[:, 8:16] = inp["L0_attn_ln_b"][l].astype(np.float32).reshape(DM, P).T
        cw = inp["L0_attn_conv_w"][l].astype(np.float32).reshape(D_INNER, 4)
        sp[:, 16:80] = cw.reshape(EC, P, 4).transpose(1, 0, 2).reshape(P, 64)
        sp[:, 80:96] = inp["L0_attn_conv_b"][l].astype(np.float32).reshape(EC, P).T
        sp[:, 96:112] = inp["L0_attn_dt_proj_b"][l].astype(np.float32).reshape(EC, P).T
        sp[:, 112:128] = inp["L0_attn_D_skip"][l].astype(np.float32).reshape(EC, P).T
        al = inp["L0_attn_A_log"][l].astype(np.float32).reshape(EC, P, NST).transpose(1, 0, 2)
        sp[:, 128:384] = al.reshape(P, 256)
        out[f"small{l}"] = sp
    out["wgT"] = np.ascontiguousarray(inp["L0_ffn_Wg"].astype(np.float32).T)
    for e in range(NE):
        w1 = np.ascontiguousarray(inp["L0_ffn_W1"][e].astype(np.float32).T)
        out[f"w1T{e}"] = np.ascontiguousarray(w1.reshape(D_MODEL, 16, 128).transpose(1, 0, 2))
        w2 = np.ascontiguousarray(inp["L0_ffn_W2"][e].astype(np.float32).T)
        out[f"w2T{e}"] = np.ascontiguousarray(w2.reshape(D_MODEL, 16, 128).transpose(1, 0, 2))
        w3 = np.ascontiguousarray(inp["L0_ffn_W3"][e].astype(np.float32).T)
        out[f"w3T{e}"] = np.ascontiguousarray(w3.reshape(HD, DM, 128).transpose(1, 0, 2))
    return out


def kernel(**inputs):
    B, S, D = inputs["L0_attn_x"].shape
    n_tp = S // TH
    w = _prep_weights(inputs)
    nc, _ = build(n_tp)
    in_maps = []
    for core in range(8):
        b = core // 2
        m = dict(w)
        m["x_hbm"] = np.ascontiguousarray(inputs["L0_attn_x"][b].astype(np.float32).T)
        in_maps.append(m)
    res = run_bass_kernel_spmd(nc, in_maps, core_ids=list(range(8)))
    outs = [np.ascontiguousarray(res.results[2 * b]["y_hbm"].T) for b in range(B)]
    return np.stack(outs, axis=0).astype(np.float32)


# version 03 (final)
# speedup vs baseline: 1.6535x; 1.6535x over previous
import contextlib
import numpy as np
import ml_dtypes
import concourse.bass as bass
import concourse.mybir as mybir
from concourse.bass_utils import run_bass_kernel_spmd

F32 = mybir.dt.float32
BF16 = mybir.dt.bfloat16
AF = mybir.ActivationFunctionType
ALU = mybir.AluOpType
AX = mybir.AxisListType

P = 128
D_MODEL = 1024
D_INNER = 2048
DM = 8
EC = 16
R = 64
NST = 16
NL = 4
NE = 8
HD = 2048
TH = 1024
TI = 512
ECL = 8
NEL = 4
GN = 2
NGH = 4
EPS = 1e-5


class Cnt:
    def __init__(self, h):
        self.h = h
        self.n = 0

    def inc(self, instr, k=1):
        instr.then_inc(self.h, k)
        self.n += k
        return self.n


def build(n_tp, taps=()):
    s_core = n_tp * TH
    nc = bass.Bass("TRN2", target_bir_lowering=False, detect_race_conditions=False)
    taps = set(taps)
    tap_seen = {}

    x_hbm = nc.dram_tensor("x_hbm", [D_MODEL, s_core], F32, kind="ExternalInput").ap()
    y_hbm = nc.dram_tensor("y_hbm", [D_MODEL, s_core], F32, kind="ExternalOutput").ap()
    in_wT = [nc.dram_tensor(f"in_wT{l}", [16, D_MODEL, 128], F32, kind="ExternalInput").ap() for l in range(NL)]
    out_wT = [nc.dram_tensor(f"out_wT{l}", [DM, D_INNER, 128], F32, kind="ExternalInput").ap() for l in range(NL)]
    xp_wT = [nc.dram_tensor(f"xp_wT{l}", [ECL, 128, 96], F32, kind="ExternalInput").ap() for l in range(NL)]
    dt_wT = [nc.dram_tensor(f"dt_wT{l}", [ECL, R, 128], F32, kind="ExternalInput").ap() for l in range(NL)]
    small = [nc.dram_tensor(f"small{l}", [P, 384], F32, kind="ExternalInput").ap() for l in range(NL)]
    wgT = nc.dram_tensor("wgT", [D_MODEL, 8], F32, kind="ExternalInput").ap()
    wsel = nc.dram_tensor("wsel", [8, NEL * 128], F32, kind="ExternalInput").ap()
    cc_dbl_i = nc.dram_tensor("cc_dbl_i", [96, TH], F32, kind="Internal").ap()
    cc_dbl_o = nc.dram_tensor("cc_dbl_o", [96, TH], F32, kind="Internal").ap()
    cc_yg_i = nc.dram_tensor("cc_yg_i", [P, 8 * TI], BF16, kind="Internal").ap()
    cc_yg_o = nc.dram_tensor("cc_yg_o", [2 * P, 8 * TI], BF16, kind="Internal").ap()
    cc_ym_i = nc.dram_tensor("cc_ym_i", [P, DM * TH], BF16, kind="Internal").ap()
    cc_ym_o = nc.dram_tensor("cc_ym_o", [P, DM * TH], BF16, kind="Internal").ap()
    RG = [[0, 1], [2, 3], [4, 5], [6, 7]]
    w1T = [nc.dram_tensor(f"w1T{e}", [16, D_MODEL, 128], F32, kind="ExternalInput").ap() for e in range(NEL)]
    w2T = [nc.dram_tensor(f"w2T{e}", [16, D_MODEL, 128], F32, kind="ExternalInput").ap() for e in range(NEL)]
    w3T = [nc.dram_tensor(f"w3T{e}", [DM, HD, 128], F32, kind="ExternalInput").ap() for e in range(NEL)]

    with contextlib.ExitStack() as stk:
        E = stk.enter_context
        x_t = E(nc.sbuf_tensor("x_t", [P, DM, TH], BF16))
        z_t = E(nc.sbuf_tensor("z_t", [P, ECL + 3, TH], BF16))
        xc_t = E(nc.sbuf_tensor("xc_t", [P, EC, TH], BF16))
        xin_c = E(nc.sbuf_tensor("xin_c", [P, 2, TH + 3], BF16))
        dA_t = E(nc.sbuf_tensor("dA_t", [P, 2, GN, TI], F32))
        brep = E(nc.sbuf_tensor("brep", [P, 8, TI], BF16))
        crep = E(nc.sbuf_tensor("crep", [P, 8, TI], BF16))
        xn_t = E(nc.sbuf_tensor("xn_t", [P, DM, TH], BF16))
        y_g = E(nc.sbuf_tensor("y_g", [P, EC, TI], BF16))
        b_t = E(nc.sbuf_tensor("b_t", [P, GN, TI], BF16))
        h_t = E(nc.sbuf_tensor("h_t", [P, GN, TI], BF16))
        hc_t = E(nc.sbuf_tensor("hc_t", [P, GN, TI], BF16))
        ya_t = E(nc.sbuf_tensor("ya_t", [P, 2, TI], BF16))
        yt_t = E(nc.sbuf_tensor("yt_t", [P, 2, TI], BF16))
        mubb, rsbb = None, None
        dts = E(nc.sbuf_tensor("dts", [P, 6, TI], F32))
        ost = E(nc.sbuf_tensor("ost", [P, 2, TI], F32))
        wst = E(nc.sbuf_tensor("wst", [P, 1024], F32))
        wbf = E(nc.sbuf_tensor("wbf", [P, 2, 2048], BF16))
        dtrb = E(nc.sbuf_tensor("dtrb", [R, TH], BF16))
        dtwb = E(nc.sbuf_tensor("dtwb", [R, ECL * 128], BF16))
        brow = E(nc.sbuf_tensor("brow", [NST, TH], BF16))
        crow = E(nc.sbuf_tensor("crow", [NST, TH], BF16))
        smt = E(nc.sbuf_tensor("smt", [P, NL, 384], F32))
        carry = E(nc.sbuf_tensor("carry", [P, NL, ECL, NST], F32))
        halo = E(nc.sbuf_tensor("halo", [P, NL, ECL, 3], BF16))
        idn = E(nc.sbuf_tensor("idn", [P, P], F32))
        ones = E(nc.sbuf_tensor("ones", [P, 2], F32))
        onesb = E(nc.sbuf_tensor("onesb", [P, 2], BF16))
        onesr = E(nc.sbuf_tensor("onesr", [1, P], F32))
        ioc = E(nc.sbuf_tensor("ioc", [P, 1], F32))
        diag = E(nc.sbuf_tensor("diag", [P, 4, P], BF16))
        sel = E(nc.sbuf_tensor("sel", [NST, NST, P], BF16))
        gt_t = E(nc.sbuf_tensor("gt_t", [P, DM, 10], F32))
        wtok = E(nc.sbuf_tensor("wtok", [P, DM, 8], F32))
        rt_t = E(nc.sbuf_tensor("rt_t", [P, 4, 8], F32))
        gg = E(nc.sbuf_tensor("gg", [P, 8, 8], F32))
        ggm = E(nc.sbuf_tensor("ggm", [P, 4, 8], F32))
        wfm = E(nc.sbuf_tensor("wfm", [8, DM * P], BF16))
        wselb = E(nc.sbuf_tensor("wselb", [8, NEL, 128], BF16))
        dtr_f = E(nc.sbuf_tensor("dtr_f", [R, TH], F32))
        bb_f = E(nc.sbuf_tensor("bb_f", [NST, TH], F32))
        cc_f = E(nc.sbuf_tensor("cc_f", [NST, TH], F32))

        ps = [E(nc.psum_tensor(f"ps{i}", [P, TI], F32)) for i in range(8)]

        mubb, rsbb = ya_t, yt_t
        s_dma = Cnt(E(nc.semaphore("s_dma")))
        s_sm = Cnt(E(nc.semaphore("s_sm")))
        s_xd = [Cnt(E(nc.semaphore("s_xd0"))), Cnt(E(nc.semaphore("s_xd1")))]
        s_ou = [Cnt(E(nc.semaphore("s_ou0"))), Cnt(E(nc.semaphore("s_ou1")))]
        s_tap = Cnt(E(nc.semaphore("s_tap")))
        s_cc = Cnt(E(nc.semaphore("s_cc")))
        s_pe = Cnt(E(nc.semaphore("s_pe")))
        s_act = Cnt(E(nc.semaphore("s_act")))
        s_dve = Cnt(E(nc.semaphore("s_dve")))
        s_pool = Cnt(E(nc.semaphore("s_pool")))
        block = E(nc.Block())

        SY, GP, TE, VE, SC = block.sync, block.gpsimd, block.tensor, block.vector, block.scalar

        free = {}

        def need(eng, *keys):
            for key in keys:
                for sem, val in free.pop(key, ()):
                    eng.wait_ge(sem.h, val)

        def rel(key, *sems):
            free[key] = [(s, s.n) for s in sems]

        def dbg(name, ap):
            if name not in taps or name in tap_seen:
                return
            t = nc.dram_tensor("dbg_" + name, list(ap.shape), ap.dtype,
                               kind="ExternalOutput").ap()
            tap_seen[name] = True
            pe_m, act_m, dve_m, pool_m = s_pe.n, s_act.n, s_dve.n, s_pool.n

            def f(gpsimd):
                gpsimd.wait_ge(s_pe.h, pe_m)
                gpsimd.wait_ge(s_act.h, act_m)
                gpsimd.wait_ge(s_dve.h, dve_m)
                gpsimd.wait_ge(s_pool.h, pool_m)
                s_tap.inc(gpsimd.dma_start(out=t, in_=ap), 16)
            GP(f)
            tv = s_tap.n

            def f2(eng):
                eng.wait_ge(s_tap.h, tv)
            TE(f2)
            VE(f2)
            SC(f2)
            SY(f2)

        def lnw(l):
            return smt[:, l, 0:8]

        def lnb(l):
            return smt[:, l, 8:16]

        def cwv(l):
            return smt[:, l, 16:48].rearrange("p (c k) -> p c k", c=ECL)

        def cbv(l):
            return smt[:, l, 48:56]

        def dtb(l):
            return smt[:, l, 56:64]

        def dpv(l):
            return smt[:, l, 64:72]

        def A_v(l):
            return smt[:, l, 72:200].rearrange("p (c n) -> p c n", c=ECL)

        dts_flat = dts[:].rearrange("p a t -> p (a t)")
        mu_row = dts_flat[0:1, 0:TH]
        m2_row = dts_flat[0:1, TH:2 * TH]
        var_row = dts_flat[0:1, 2 * TH:3 * TH]
        rs_row = var_row

        def f(sync):
            for l in range(NL):
                s_sm.inc(sync.dma_start(out=smt[:, l, :], in_=small[l]), 16)
        SY(f)
        sm_ld = s_sm.n

        def f(gpsimd):
            gpsimd.iota(idn[:], pattern=[[1, P]], base=0, channel_multiplier=0,
                        allow_small_or_imprecise_dtypes=True)
            gpsimd.iota(ioc[:], pattern=[[0, 1]], base=0, channel_multiplier=1,
                        allow_small_or_imprecise_dtypes=True)
            gpsimd.memset(carry[:], 0.0)
            gpsimd.memset(ones[:], 1.0)
            gpsimd.memset(onesb[:], 1.0)
            s_pool.inc(gpsimd.memset(onesr[:], 1.0))
        GP(f)
        pool_init = s_pool.n

        def f(vector):
            vector.wait_ge(s_pool.h, pool_init)
            vector.tensor_tensor(idn[:], idn[:], ioc[:].broadcast_to([P, P]),
                                 op=ALU.is_equal)
            for n in range(NST):
                vector.tensor_scalar(sel[:, n, :], ioc[0:NST, :].broadcast_to([NST, P]),
                                     float(n), None, op0=ALU.is_equal)
            s_dve.inc(vector.memset(halo[:], 0.0))
        VE(f)
        init_done = s_dve.n

        def f(scalar):
            scalar.wait_ge(s_sm.h, sm_ld)
            for l in range(NL):
                s_act.inc(scalar.activation(smt[:, l, 72:200], smt[:, l, 72:200], AF.Exp))
        SC(f)
        act_A = s_act.n

        def f(vector):
            vector.wait_ge(s_act.h, act_A)
            for l in range(NL):
                s_dve.inc(vector.tensor_scalar(smt[:, l, 72:200], smt[:, l, 72:200],
                                               -1.0, None, op0=ALU.mult))
        VE(f)

        def f(sync):
            s_sm.inc(sync.dma_start(
                out=dtr_f[0:8, 0:NEL * 128], in_=wsel), 16)
        SY(f)
        wsel_ld = s_sm.n

        def f(vector, wsel_ld=wsel_ld):
            vector.wait_ge(s_sm.h, wsel_ld)
            s_dve.inc(vector.tensor_scalar(
                wselb[:].rearrange("p a b -> p (a b)"), dtr_f[0:8, 0:NEL * 128],
                1.0, None, op0=ALU.mult))
        VE(f)
        wsel_done = s_dve.n

        def layernorm(src, l, apply_w, dst, to_ost=False, out_writer=None):
            act0 = s_act.n
            pe0 = s_pe.n
            src_dve = s_dve.n

            def f(scalar):
                scalar.wait_ge(s_dve.h, src_dve)
                need(scalar, "dts_sp", "dtf", "dts_rows")
                for tt in range(2):
                    for c in range(DM):
                        k = tt * DM + c
                        if k >= 2:
                            scalar.wait_ge(s_pe.h, pe0 + k - 1)
                        s_act.inc(scalar.activation(
                            dts[:, 4 + (k % 2), :], src[c][:, tt * TI:(tt + 1) * TI],
                            AF.Square))
            SC(f)

            def f(tensor):
                tensor.wait_ge(s_pool.h, pool_init)
                tensor.wait_ge(s_dve.h, src_dve)
                need(tensor, "ps0", "ps1", "ps2", "ps3")
                for tt in range(2):
                    for c in range(DM):
                        k = tt * DM + c
                        tensor.matmul(ps[0 + tt][0:1, :], onesb[:, 0:1],
                                      src[c][:, tt * TI:(tt + 1) * TI],
                                      start=(c == 0), stop=(c == DM - 1))
                        tensor.wait_ge(s_act.h, act0 + k + 1)
                        mm = tensor.matmul(ps[2 + tt][0:1, :], ones[:, 0:1],
                                           dts[:, 4 + (k % 2), :],
                                           start=(c == 0), stop=(c == DM - 1))
                        s_pe.inc(mm)
            TE(f)
            pe_sums = s_pe.n

            def f(vector):
                vector.wait_ge(s_pe.h, pe_sums)
                for tt in range(2):
                    sl = slice(tt * TI, (tt + 1) * TI)
                    vector.tensor_scalar(mu_row[:, sl], ps[0 + tt][0:1, :], 1.0 / D_MODEL,
                                         None, op0=ALU.mult)
                    vector.tensor_scalar(m2_row[:, sl], ps[2 + tt][0:1, :], 1.0 / D_MODEL,
                                         None, op0=ALU.mult)
                vector.tensor_tensor(var_row[:], mu_row[:], mu_row[:], op=ALU.mult)
                vector.tensor_tensor(var_row[:], m2_row[:], var_row[:], op=ALU.subtract)
                s_dve.inc(vector.tensor_scalar(var_row[:], var_row[:], 1.0, EPS,
                                               op0=ALU.mult, op1=ALU.add))
            VE(f)
            rel("ps0", s_dve)
            rel("ps1", s_dve)
            rel("ps2", s_dve)
            rel("ps3", s_dve)
            dve_var = s_dve.n

            def f(scalar):
                scalar.wait_ge(s_dve.h, dve_var)
                s_act.inc(scalar.activation(var_row[:], var_row[:], AF.Sqrt))
            SC(f)
            act_sd = s_act.n

            def f(vector):
                vector.wait_ge(s_act.h, act_sd)
                s_dve.inc(vector.reciprocal(rs_row[:], var_row[:]))
            VE(f)
            dve_rs = s_dve.n

            def f(tensor):
                tensor.wait_ge(s_dve.h, dve_rs)
                need(tensor, "ps4", "ps5", "ps6", "ps7")
                for tt in range(2):
                    sl = slice(tt * TI, (tt + 1) * TI)
                    tensor.matmul(ps[4 + tt][:, :], onesr[:], mu_row[:, sl],
                                  start=True, stop=True)
                    mm = tensor.matmul(ps[6 + tt][:, :], onesr[:], rs_row[:, sl],
                                       start=True, stop=True)
                s_pe.inc(mm)
            TE(f)
            rel("dts_rows", s_pe)
            pe_bc = s_pe.n

            def f(scalar):
                scalar.wait_ge(s_pe.h, pe_bc)
                scalar.wait_ge(s_dve.h, src_dve)
                need(scalar, "mubb", "ytt")
                for tt in range(2):
                    scalar.activation(mubb[:, tt, :], ps[4 + tt][:, :], AF.Copy)
                    s_act.inc(scalar.activation(rsbb[:, tt, :], ps[6 + tt][:, :], AF.Copy))
            SC(f)
            rel("ps4", s_act)
            rel("ps5", s_act)
            rel("ps6", s_act)
            rel("ps7", s_act)
            act_bc = s_act.n

            writes = []
            if to_ost:
                for c in range(DM):
                    for tt in range(2):
                        pp = (c * 2 + tt) % 2

                        def f(vector, c=c, tt=tt, pp=pp):
                            if c == 0 and tt == 0:
                                vector.wait_ge(s_act.h, act_bc)
                            sl = slice(tt * TI, (tt + 1) * TI)
                            vector.tensor_tensor(hc_t[:, 0, :], src[c][:, sl],
                                                 mubb[:, tt, :], op=ALU.subtract)
                            need(vector, f"ostw{pp}")
                            s_dve.inc(vector.tensor_tensor(ost[:, pp, :], hc_t[:, 0, :],
                                                           rsbb[:, tt, :], op=ALU.mult))
                        VE(f)
                        writes.append((c, tt, pp, s_dve.n))
                        if out_writer is not None:
                            out_writer(c, tt, pp, s_dve.n)
                rel("mubb", s_dve)
                return s_dve.n, writes

            def f(vector):
                vector.wait_ge(s_act.h, act_bc)
                for c in range(DM):
                    for tt in range(2):
                        sl = slice(tt * TI, (tt + 1) * TI)
                        vector.tensor_tensor(hc_t[:, 0, :], src[c][:, sl], mubb[:, tt, :],
                                             op=ALU.subtract)
                        if apply_w:
                            vector.tensor_tensor(hc_t[:, 1, :], hc_t[:, 0, :],
                                                 rsbb[:, tt, :], op=ALU.mult)
                            mm = vector.tensor_scalar(dst[c][:, sl], hc_t[:, 1, :],
                                                      lnw(l)[:, c:c + 1], lnb(l)[:, c:c + 1],
                                                      op0=ALU.mult, op1=ALU.add)
                        else:
                            mm = vector.tensor_tensor(dst[c][:, sl], hc_t[:, 0, :],
                                                      rsbb[:, tt, :], op=ALU.mult)
                        s_dve.inc(mm)
            VE(f)
            rel("mubb", s_dve)
            return s_dve.n, writes

        out_total = [0]
        for tp in range(n_tp):
            toff0 = tp * TH
            for c in range(DM):
                for tt in range(2):
                    pp = (c * 2 + tt) % 2

                    def f(sync, c=c, tt=tt, pp=pp):
                        need(sync, f"ostr{pp}")
                        s_xd[pp].inc(sync.dma_start(
                            out=ost[:, pp, :],
                            in_=x_hbm[c * P:(c + 1) * P,
                                      toff0 + tt * TI:toff0 + (tt + 1) * TI]), 16)
                    SY(f)
                    xl = s_xd[pp].n

                    def f(vector, c=c, tt=tt, pp=pp, xl=xl):
                        vector.wait_ge(s_xd[pp].h, xl)
                        s_dve.inc(vector.tensor_scalar(
                            x_t[:, c, tt * TI:(tt + 1) * TI], ost[:, pp, :], 1.0, None,
                            op0=ALU.mult))
                    VE(f)
                    rel(f"ostr{pp}", s_dve)
            dbg(f"xload{tp}", x_t[:])

            for l in range(NL):
                xch = [x_t[:, c, :] for c in range(DM)]
                xnch = [xn_t[:, c, :] for c in range(DM)]
                ln_done, _ = layernorm(xch, l, True, xnch)
                dbg(f"xn{tp}_{l}", xn_t[:])

                for ec in range(16):
                    pp = ec % 2

                    def f(sync, l=l, ec=ec):
                        need(sync, "wst")
                        s_dma.inc(sync.dma_start(
                            out=wst[:, 0:1024].rearrange("p (c e) -> p c e", c=DM),
                            in_=in_wT[l][ec].rearrange("(c p) e -> p c e", p=P)), 16)
                    SY(f)
                    wld = s_dma.n

                    def f(vector, wld=wld, pp=pp):
                        vector.wait_ge(s_dma.h, wld)
                        need(vector, f"wbf{pp}")
                        s_dve.inc(vector.tensor_scalar(wbf[:, pp, 0:1024], wst[:, 0:1024],
                                                       1.0, None, op0=ALU.mult))
                    VE(f)
                    rel("wst", s_dve)
                    wcast = s_dve.n

                    def f(tensor, wcast=wcast, ln_done=ln_done, pp=pp):
                        tensor.wait_ge(s_dve.h, max(wcast, ln_done))
                        wv = wbf[:, pp, 0:1024].rearrange("p (c e) -> p c e", c=DM)
                        for tt in range(2):
                            need(tensor, f"ps{2 * pp + tt}")
                            for c in range(DM):
                                mm = tensor.matmul(ps[2 * pp + tt][:, :], wv[:, c, :],
                                                   xn_t[:, c, tt * TI:(tt + 1) * TI],
                                                   start=(c == 0), stop=(c == DM - 1))
                            s_pe.inc(mm)
                    TE(f)
                    rel(f"wbf{pp}", s_pe)
                    mm1 = s_pe.n - 1

                    if ec >= 8:
                        e = ec - 8

                        def f(scalar, mm1=mm1, e=e, pp=pp):
                            if e <= 1:
                                need(scalar, "zs1")
                            if e == 2:
                                need(scalar, "zwbc")
                            for tt in range(2):
                                scalar.wait_ge(s_pe.h, mm1 + tt)
                                s_act.inc(scalar.activation(
                                    z_t[:, e, tt * TI:(tt + 1) * TI],
                                    ps[2 * pp + tt][:, :], AF.Silu))
                        SC(f)
                        rel(f"ps{2 * pp}", s_act)
                        rel(f"ps{2 * pp + 1}", s_act)
                    else:
                        xp = ec % 2

                        def f(scalar, mm1=mm1, xp=xp, pp=pp):
                            need(scalar, f"xin{xp}")
                            for tt in range(2):
                                scalar.wait_ge(s_pe.h, mm1 + tt)
                                s_act.inc(scalar.activation(
                                    xin_c[:, xp, 3 + tt * TI:3 + (tt + 1) * TI],
                                    ps[2 * pp + tt][:, :], AF.Copy))
                        SC(f)
                        rel(f"ps{2 * pp}", s_act)
                        rel(f"ps{2 * pp + 1}", s_act)
                        evac = s_act.n

                        def f(vector, l=l, ec=ec, xp=xp, evac=evac):
                            vector.wait_ge(s_act.h, evac)
                            vector.tensor_scalar(xin_c[:, xp, 0:3], halo[:, l, ec, :],
                                                 1.0, None, op0=ALU.mult)
                            vector.tensor_scalar(halo[:, l, ec, :],
                                                 xin_c[:, xp, TH:TH + 3], 1.0, None,
                                                 op0=ALU.mult)
                            need(vector, "diag")
                            for k in range(4):
                                s_dve.inc(vector.tensor_scalar(
                                    diag[:, k, :], idn[:], cwv(l)[:, ec, k:k + 1],
                                    None, op0=ALU.mult))
                        VE(f)
                        dve_cd = s_dve.n

                        def f(tensor, dve_cd=dve_cd, xp=xp):
                            tensor.wait_ge(s_dve.h, dve_cd)
                            for tt in range(2):
                                need(tensor, f"ps{4 + 2 * xp + tt}")
                                for k in range(4):
                                    mm = tensor.matmul(
                                        ps[4 + 2 * xp + tt][:, :], diag[:, k, :],
                                        xin_c[:, xp, k + tt * TI:k + tt * TI + TI],
                                        start=(k == 0), stop=(k == 3))
                                s_pe.inc(mm)
                        TE(f)
                        rel("diag", s_pe)
                        rel(f"xin{xp}", s_pe, s_dve)
                        cv1 = s_pe.n - 1

                        def f(scalar, cv1=cv1, ec=ec, l=l, xp=xp):
                            need(scalar, f"gfull{ec}")
                            for tt in range(2):
                                scalar.wait_ge(s_pe.h, cv1 + tt)
                                s_act.inc(scalar.activation(
                                    xc_t[:, ec, tt * TI:(tt + 1) * TI],
                                    ps[4 + 2 * xp + tt][:, :], AF.Silu,
                                    bias=cbv(l)[:, ec:ec + 1]))
                        SC(f)
                        rel(f"ps{4 + 2 * xp}", s_act)
                        rel(f"ps{4 + 2 * xp + 1}", s_act)
                xc_all = s_act.n
                dbg(f"xc{tp}_{l}", xc_t[:])
                dbg(f"z{tp}_{l}", z_t[:])

                xpc = 0
                def f(sync, l=l):
                    need(sync, "wst")
                    s_dma.inc(sync.dma_start(
                        out=wst[:, 0:768].rearrange("p (c r) -> p c r", c=8),
                        in_=xp_wT[l].rearrange("c p r -> p c r")), 16)
                SY(f)
                xpld = s_dma.n

                def f(vector, xpld=xpld):
                    vector.wait_ge(s_dma.h, xpld)
                    need(vector, "wbf0")
                    s_dve.inc(vector.tensor_scalar(
                        wbf[:, 0, 0:768], wst[:, 0:768], 1.0, None, op0=ALU.mult))
                VE(f)
                rel("wst", s_dve)
                xpc = s_dve.n

                def f(tensor, xpc=xpc, xc_all=xc_all):
                    tensor.wait_ge(s_dve.h, xpc)
                    tensor.wait_ge(s_act.h, xc_all)
                    wv = wbf[:, 0, 0:ECL * 96].rearrange("p (c r) -> p c r", c=ECL)
                    for tt in range(2):
                        sl = slice(tt * TI, (tt + 1) * TI)
                        need(tensor, f"ps{0 + tt}", f"ps{2 + tt}", f"ps{4 + tt}")
                        for c in range(ECL):
                            tensor.matmul(ps[0 + tt][0:R, :], wv[:, c, 0:R],
                                          xc_t[:, c, sl], start=(c == 0),
                                          stop=(c == ECL - 1))
                        for c in range(ECL):
                            tensor.matmul(ps[2 + tt][0:NST, :], wv[:, c, R:R + NST],
                                          xc_t[:, c, sl], start=(c == 0),
                                          stop=(c == ECL - 1))
                        for c in range(ECL):
                            mm = tensor.matmul(ps[4 + tt][0:NST, :], wv[:, c, R + NST:96],
                                               xc_t[:, c, sl], start=(c == 0),
                                               stop=(c == ECL - 1))
                        s_pe.inc(mm)
                TE(f)
                rel("wbf0", s_pe)
                dbl1 = s_pe.n - 1

                def f(scalar, dbl1=dbl1):
                    need(scalar, "bcrows", "dblstage")
                    for tt in range(2):
                        sl = slice(tt * TI, (tt + 1) * TI)
                        scalar.wait_ge(s_pe.h, dbl1 + tt)
                        scalar.activation(dtr_f[:, sl], ps[0 + tt][0:R, :], AF.Copy)
                        scalar.activation(bb_f[:, sl], ps[2 + tt][0:NST, :], AF.Copy)
                        s_act.inc(scalar.activation(cc_f[:, sl], ps[4 + tt][0:NST, :],
                                                    AF.Copy))
                SC(f)
                for i in range(6):
                    rel(f"ps{i}", s_act)
                dbl_part = s_act.n

                def f(gpsimd, dbl_part=dbl_part):
                    gpsimd.wait_ge(s_act.h, dbl_part)
                    gpsimd.dma_start(out=cc_dbl_i[0:R, :], in_=dtr_f[:]).then_inc(s_cc.h, 16)
                    gpsimd.dma_start(out=cc_dbl_i[R:R + NST, :], in_=bb_f[:]).then_inc(s_cc.h, 16)
                    gpsimd.dma_start(out=cc_dbl_i[R + NST:96, :], in_=cc_f[:]).then_inc(s_cc.h, 16)
                    s_cc.n += 48
                    gpsimd.wait_ge(s_cc.h, s_cc.n)
                    s_cc.inc(gpsimd.collective_compute(
                        "AllReduce", ALU.add, replica_groups=RG,
                        ins=[cc_dbl_i.opt()], outs=[cc_dbl_o.opt()]))
                    gpsimd.wait_ge(s_cc.h, s_cc.n)
                    gpsimd.dma_start(out=dtr_f[:], in_=cc_dbl_o[0:R, :]).then_inc(s_cc.h, 16)
                    gpsimd.dma_start(out=bb_f[:], in_=cc_dbl_o[R:R + NST, :]).then_inc(s_cc.h, 16)
                    gpsimd.dma_start(out=cc_f[:], in_=cc_dbl_o[R + NST:96, :]).then_inc(s_cc.h, 16)
                    s_cc.n += 48
                GP(f)
                dbl_cc = s_cc.n

                def f(vector, dbl_cc=dbl_cc):
                    vector.wait_ge(s_cc.h, dbl_cc)
                    vector.tensor_scalar(dtrb[:], dtr_f[:], 1.0, None, op0=ALU.mult)
                    vector.tensor_scalar(brow[:], bb_f[:], 1.0, None, op0=ALU.mult)
                    s_dve.inc(vector.tensor_scalar(crow[:], cc_f[:], 1.0, None,
                                                   op0=ALU.mult))
                VE(f)
                rel("dblstage", s_dve)
                bc_ev = s_dve.n
                dbg(f"dtr{tp}_{l}", dtrb[:])
                dbg(f"brow{tp}_{l}", brow[:])

                def f(sync, l=l):
                    need(sync, "wst")
                    s_dma.inc(sync.dma_start(
                        out=wst[0:R, 0:1024].rearrange("p (c e) -> p c e", c=8),
                        in_=dt_wT[l].rearrange("c p e -> p c e")), 16)
                SY(f)
                dtld = s_dma.n

                def f(vector, dtld=dtld):
                    vector.wait_ge(s_dma.h, dtld)
                    need(vector, "dtwb")
                    s_dve.inc(vector.tensor_scalar(
                        dtwb[:], wst[0:R, 0:1024], 1.0, None, op0=ALU.mult))
                VE(f)
                rel("wst", s_dve)
                dtwc = s_dve.n
                dtw_v = dtwb[:].rearrange("p (c e) -> p c e", c=ECL)

                for ti in range(2):
                    toff = ti * TI
                    for nh in range(2):
                        ract0 = s_act.n

                        def f(tensor, ti=ti, nh=nh, bc_ev=bc_ev, ract0=ract0):
                            tensor.wait_ge(s_dve.h, bc_ev)
                            tensor.wait_ge(s_dve.h, init_done)
                            for j in range(8):
                                n = nh * 8 + j
                                bb_, cb_ = (0, 1) if j % 2 == 0 else (2, 3)
                                need(tensor, f"ps{bb_}", f"ps{cb_}")
                                if j >= 2:
                                    tensor.wait_ge(s_act.h, ract0 + j - 1)
                                tensor.matmul(ps[bb_][:, :], sel[:, n, :],
                                              brow[:, toff:toff + TI], start=True, stop=True)
                                mm = tensor.matmul(ps[cb_][:, :], sel[:, n, :],
                                                   crow[:, toff:toff + TI], start=True,
                                                   stop=True)
                                s_pe.inc(mm)
                        TE(f)
                        rep_mm = s_pe.n

                        def f(scalar, rep_mm=rep_mm):
                            need(scalar, "brep")
                            for j in range(8):
                                scalar.wait_ge(s_pe.h, rep_mm - 8 + j + 1)
                                bb_, cb_ = (0, 1) if j % 2 == 0 else (2, 3)
                                scalar.activation(brep[:, j, :], ps[bb_][:, :], AF.Copy)
                                s_act.inc(scalar.activation(crep[:, j, :], ps[cb_][:, :],
                                                            AF.Copy))
                        SC(f)
                        for i in range(4):
                            rel(f"ps{i}", s_act)
                        rep_ev = s_act.n
                        if l == 0 and ti == 0 and nh == 0:
                            dbg("brep00", brep[:])
                            dbg("crep00", crep[:])
                        if l == 0 and ti == 1 and nh == 0:
                            dbg("brep10", brep[:])
                            dbg("crep10", crep[:])

                        for dc in range(ECL):
                            def f(tensor, dc=dc, ti=ti, dtwc=dtwc):
                                tensor.wait_ge(s_dve.h, dtwc)
                                need(tensor, "ps6")
                                s_pe.inc(tensor.matmul(ps[6][:, :], dtw_v[:, dc, :],
                                                       dtrb[:, toff:toff + TI],
                                                       start=True, stop=True))
                            TE(f)
                            dtmm = s_pe.n

                            def f(scalar, dtmm=dtmm, l=l, dc=dc):
                                scalar.wait_ge(s_pe.h, dtmm)
                                need(scalar, "dts_sp")
                                scalar.activation(dts[:, 1, :], ps[6][:, :], AF.Abs,
                                                  bias=dtb(l)[:, dc:dc + 1])
                                scalar.activation(dts[:, 2, :], dts[:, 1, :], AF.Exp,
                                                  scale=-1.0)
                                scalar.activation(dts[:, 3, :], dts[:, 2, :], AF.Ln,
                                                  bias=1.0)
                                s_act.inc(scalar.activation(dts[:, 4, :], ps[6][:, :],
                                                            AF.Relu,
                                                            bias=dtb(l)[:, dc:dc + 1]))
                            SC(f)
                            rel("ps6", s_act)
                            spd = s_act.n

                            def f(vector, spd=spd, dc=dc, ti=ti):
                                vector.wait_ge(s_act.h, spd)
                                need(vector, "dtf")
                                vector.tensor_tensor(dts[:, 5, :], dts[:, 3, :],
                                                     dts[:, 4, :], op=ALU.add)
                                need(vector, "ytt")
                                vector.tensor_scalar(yt_t[:, 0, :], dts[:, 5, :], 1.0,
                                                     None, op0=ALU.mult)
                                s_dve.inc(vector.tensor_tensor(
                                    yt_t[:, 1, :], yt_t[:, 0, :],
                                    xc_t[:, dc, toff:toff + TI], op=ALU.mult))
                            VE(f)
                            rel("dts_sp", s_dve)
                            dtrdy = s_dve.n
                            dtu_v = yt_t[:, 1, :]
                            if l == 0 and ti == 0 and nh == 0 and dc == 0:
                                dbg("dt000", dts[:, 5, :])
                                dbg("dtu000", yt_t[:, 1, :])
                            if l == 0 and ti == 1 and nh == 0 and dc == 0:
                                dbg("dt100", dts[:, 5, :])
                                dbg("dtu100", yt_t[:, 1, :])

                            for g in range(NGH):
                                qp = g % 2

                                def f(scalar, g=g, qp=qp, dc=dc, l=l, nh=nh, dtrdy=dtrdy):
                                    scalar.wait_ge(s_dve.h, dtrdy)
                                    need(scalar, f"dA{qp}")
                                    for j in range(GN):
                                        n = nh * 8 + g * GN + j
                                        s_act.inc(scalar.activation(
                                            dA_t[:, qp, j, :], dts[:, 5, :], AF.Exp,
                                            scale=A_v(l)[:, dc, n:n + 1]))
                                SC(f)
                                dAd = s_act.n

                                def f(vector, g=g, qp=qp, dc=dc, l=l, nh=nh, ti=ti,
                                      dAd=dAd, rep_ev=rep_ev):
                                    vector.wait_ge(s_act.h, max(dAd, rep_ev))
                                    j0 = g * GN
                                    s_dve.inc(vector.tensor_tensor(
                                        b_t[:],
                                        dtu_v.unsqueeze(1).broadcast_to([P, GN, TI]),
                                        brep[:, j0:j0 + GN, :], op=ALU.mult))
                                    vector.wait_ge(s_dve.h, s_dve.n)
                                    for j in range(GN):
                                        n = nh * 8 + g * GN + j
                                        mm = vector.tensor_tensor_scan(
                                            h_t[:, j, :], dA_t[:, qp, j, :], b_t[:, j, :],
                                            carry[:, l, dc, n:n + 1],
                                            op0=ALU.mult, op1=ALU.add)
                                    s_dve.inc(mm)
                                    vector.wait_ge(s_dve.h, s_dve.n)
                                    vector.tensor_scalar(
                                        carry[:, l, dc, nh * 8 + j0:nh * 8 + j0 + GN],
                                        h_t[:, :, TI - 1], 1.0, None, op0=ALU.mult)
                                    vector.tensor_tensor(hc_t[:], h_t[:],
                                                         crep[:, j0:j0 + GN, :],
                                                         op=ALU.mult)
                                    if g == 0:
                                        mm = vector.tensor_tensor(ya_t[:, 0, :],
                                                                  hc_t[:, 0, :],
                                                                  hc_t[:, 1, :], op=ALU.add)
                                    else:
                                        vector.tensor_tensor(ya_t[:, 1, :], hc_t[:, 0, :],
                                                             hc_t[:, 1, :], op=ALU.add)
                                        mm = vector.tensor_tensor(ya_t[:, 0, :],
                                                                  ya_t[:, 0, :],
                                                                  ya_t[:, 1, :], op=ALU.add)
                                    s_dve.inc(mm)
                                VE(f)
                                rel(f"dA{qp}", s_dve)

                            if l == 0 and ti == 0 and nh == 0 and dc == 0:
                                dbg("h000", h_t[:])
                                dbg("dA000", dA_t[:])
                            if l == 0 and ti == 1 and nh == 0 and dc == 0:
                                dbg("h100", h_t[:])
                                dbg("dA100", dA_t[:])

                            def f(vector, dc=dc, ti=ti, nh=nh, l=l, ygw=s_cc.n):
                                if nh == 0 and dc == 0:
                                    vector.wait_ge(s_cc.h, ygw)
                                if nh == 0:
                                    need(vector, "y_g")
                                    mm = vector.tensor_scalar(y_g[:, dc, :], ya_t[:, 0, :],
                                                              1.0, None, op0=ALU.mult)
                                else:
                                    vector.tensor_tensor(ya_t[:, 1, :], y_g[:, dc, :],
                                                         ya_t[:, 0, :], op=ALU.add)
                                    vector.tensor_scalar(yt_t[:, 0, :],
                                                         xc_t[:, dc, toff:toff + TI],
                                                         dpv(l)[:, dc:dc + 1], None,
                                                         op0=ALU.mult)
                                    vector.tensor_tensor(yt_t[:, 0, :], ya_t[:, 1, :],
                                                         yt_t[:, 0, :], op=ALU.add)
                                    mm = vector.tensor_tensor(
                                        y_g[:, dc, :], yt_t[:, 0, :],
                                        z_t[:, dc, toff:toff + TI], op=ALU.mult)
                                s_dve.inc(mm)
                            VE(f)
                            rel("ytt", s_dve)
                            rel("dtf", s_dve)
                            if l == 0 and ti == 0 and nh == 0 and dc == 0:
                                dbg("ya000", y_g[:, 0, :])
                        rel("brep", s_dve)
                    yg_done = s_dve.n
                    dbg(f"yg{tp}_{l}_{ti}", y_g[:])

                    def f(gpsimd, yg_done=yg_done):
                        gpsimd.wait_ge(s_dve.h, yg_done)
                        gpsimd.dma_start(
                            out=cc_yg_i,
                            in_=y_g[:, 0:8, :].rearrange("p a t -> p (a t)")
                        ).then_inc(s_cc.h, 16)
                        s_cc.n += 16
                        gpsimd.wait_ge(s_cc.h, s_cc.n)
                        s_cc.inc(gpsimd.collective_compute(
                            "AllGather", ALU.bypass, replica_groups=RG,
                            ins=[cc_yg_i.opt()], outs=[cc_yg_o.opt()]))
                        gpsimd.wait_ge(s_cc.h, s_cc.n)
                        gpsimd.dma_start(
                            out=y_g[:, 0:8, :].rearrange("p a t -> p (a t)"),
                            in_=cc_yg_o[0:P, :]).then_inc(s_cc.h, 16)
                        gpsimd.dma_start(
                            out=y_g[:, 8:16, :].rearrange("p a t -> p (a t)"),
                            in_=cc_yg_o[P:2 * P, :]).then_inc(s_cc.h, 16)
                        s_cc.n += 32
                    GP(f)
                    yg_cc = s_cc.n

                    for dm_c in range(DM):
                        pp = dm_c % 2

                        owc = 0
                        for hf in range(2):
                            def f(sync, l=l, dm_c=dm_c, hf=hf):
                                need(sync, "wst")
                                s_dma.inc(sync.dma_start(
                                    out=wst[:].rearrange("p (c e) -> p c e", c=DM),
                                    in_=out_wT[l][dm_c][hf * 1024:(hf + 1) * 1024]
                                    .rearrange("(c p) e -> p c e", p=P)), 16)
                            SY(f)
                            owld = s_dma.n

                            def f(vector, owld=owld, pp=pp, hf=hf):
                                vector.wait_ge(s_dma.h, owld)
                                if hf == 0:
                                    need(vector, f"wbf{pp}")
                                s_dve.inc(vector.tensor_scalar(
                                    wbf[:, pp, hf * 1024:(hf + 1) * 1024], wst[:], 1.0,
                                    None, op0=ALU.mult))
                            VE(f)
                            rel("wst", s_dve)
                            owc = s_dve.n

                        def f(tensor, owc=owc, yg_cc=yg_cc, pp=pp):
                            tensor.wait_ge(s_dve.h, owc)
                            tensor.wait_ge(s_cc.h, yg_cc)
                            need(tensor, f"ps{6 + pp}")
                            wv = wbf[:, pp, :].rearrange("p (c e) -> p c e", c=EC)
                            for c in range(EC):
                                mm = tensor.matmul(ps[6 + pp][:, :], wv[:, c, :],
                                                   y_g[:, c, :], start=(c == 0),
                                                   stop=(c == EC - 1))
                            s_pe.inc(mm)
                        TE(f)
                        rel(f"wbf{pp}", s_pe)
                        opmm = s_pe.n

                        def f(scalar, opmm=opmm, pp=pp):
                            scalar.wait_ge(s_pe.h, opmm)
                            need(scalar, "ytt")
                            s_act.inc(scalar.activation(yt_t[:, pp, :], ps[6 + pp][:, :],
                                                        AF.Copy))
                        SC(f)
                        rel(f"ps{6 + pp}", s_act)
                        opev = s_act.n

                        def f(vector, opev=opev, dm_c=dm_c, ti=ti, pp=pp):
                            vector.wait_ge(s_act.h, opev)
                            s_dve.inc(vector.tensor_tensor(
                                x_t[:, dm_c, toff:toff + TI], yt_t[:, pp, :],
                                x_t[:, dm_c, toff:toff + TI], op=ALU.add))
                        VE(f)
                        rel("ytt", s_dve)
                    rel("y_g", s_pe)
                rel("dtwb", s_pe)
                rel("bcrows", s_pe)
                dbg(f"xl{tp}_{l}", x_t[:])

            for c in range(DM):
                for tt in range(2):
                    pp = (c * 2 + tt) % 2

                    def f(sync, c=c, tt=tt, pp=pp):
                        need(sync, f"ostr{pp}")
                        s_xd[pp].inc(sync.dma_start(
                            out=ost[:, pp, :],
                            in_=x_hbm[c * P:(c + 1) * P,
                                      toff0 + tt * TI:toff0 + (tt + 1) * TI]), 16)
                    SY(f)
                    xl = s_xd[pp].n

                    def f(vector, c=c, tt=tt, pp=pp, xl=xl):
                        vector.wait_ge(s_xd[pp].h, xl)
                        need(vector, "ytt")
                        vector.tensor_scalar(yt_t[:, pp, :], ost[:, pp, :], 1.0, None,
                                             op0=ALU.mult)
                        s_dve.inc(vector.tensor_tensor(
                            x_t[:, c, tt * TI:(tt + 1) * TI], yt_t[:, pp, :],
                            x_t[:, c, tt * TI:(tt + 1) * TI], op=ALU.add))
                    VE(f)
                    rel(f"ostr{pp}", s_dve)
                    rel("ytt", s_dve)

            xch = [x_t[:, c, :] for c in range(DM)]
            ln1, _ = layernorm(xch, 0, False, xch)
            dbg(f"pl1_{tp}", x_t[:])

            def f(sync):
                need(sync, "wst")
                s_dma.inc(sync.dma_start(
                    out=wst[:, 0:DM * 8].rearrange("p (c e) -> p c e", c=DM),
                    in_=wgT.rearrange("(c p) e -> p c e", p=P)), 16)
            SY(f)
            wgld = s_dma.n

            def f(vector, wgld=wgld):
                vector.wait_ge(s_dma.h, wgld)
                need(vector, "wbf0")
                s_dve.inc(vector.tensor_scalar(wbf[:, 0, 0:DM * 8], wst[:, 0:DM * 8],
                                               1.0, None, op0=ALU.mult))
            VE(f)
            rel("wst", s_dve)
            wgc = s_dve.n

            def f(tensor, wgc=wgc, ln1=ln1):
                tensor.wait_ge(s_dve.h, max(wgc, ln1))
                wv = wbf[:, 0, 0:DM * 8].rearrange("p (c e) -> p c e", c=DM)
                for tt in range(2):
                    need(tensor, f"ps{0 + tt}")
                    for c in range(DM):
                        mm = tensor.matmul(ps[0 + tt][0:8, :], wv[:, c, :],
                                           x_t[:, c, tt * TI:(tt + 1) * TI],
                                           start=(c == 0), stop=(c == DM - 1))
                    s_pe.inc(mm)
            TE(f)
            rel("wbf0", s_pe)
            gmm = s_pe.n

            def f(scalar, gmm=gmm):
                need(scalar, "dts_sp", "dtf", "dts_rows")
                for tt in range(2):
                    scalar.wait_ge(s_pe.h, gmm - 1 + tt)
                    s_act.inc(scalar.activation(dts_flat[0:8, tt * TI:(tt + 1) * TI],
                                                ps[0 + tt][0:8, :], AF.Copy))
            SC(f)
            rel("ps0", s_act)
            rel("ps1", s_act)
            gev = s_act.n
            gate_fm = dts_flat[0:8, 0:TH]
            dbg(f"gfm{tp}", dts_flat[0:8, 0:TH])


            tact0 = s_act.n

            def f(tensor, gev=gev, tact0=tact0):
                tensor.wait_ge(s_act.h, gev)
                for tc in range(DM):
                    need(tensor, f"ps{2 + tc % 2}")
                    if tc >= 2:
                        tensor.wait_ge(s_act.h, tact0 + tc - 1)
                    s_pe.inc(tensor.transpose((ps[2] if tc % 2 == 0 else ps[3])[:, 0:8],
                                              gate_fm[:, tc * P:(tc + 1) * P],
                                              idn[0:8, 0:8]))
            TE(f)
            rel("dts_rows", s_pe)
            gtmm = s_pe.n

            def f(scalar, gtmm=gtmm):
                need(scalar, "gt")
                for tc in range(DM):
                    scalar.wait_ge(s_pe.h, gtmm - DM + tc + 1)
                    s_act.inc(scalar.activation(
                        gt_t[:, tc, 0:8], (ps[2] if tc % 2 == 0 else ps[3])[:, 0:8],
                        AF.Copy))
            SC(f)
            rel("ps2", s_act)
            rel("ps3", s_act)
            gtev = s_act.n
            dbg(f"gtok{tp}", gt_t[:])

            def f(vector, gtev=gtev):
                vector.wait_ge(s_act.h, gtev)

                def step(instr):
                    s_dve.inc(instr)
                    vector.wait_ge(s_dve.h, s_dve.n)
                step(vector.tensor_copy(gg[:], gt_t[:, :, 0:8].transpose([0, 2, 1])))
                step(vector.tensor_tensor(rt_t[:], gg[:, 0:4, :], gg[:, 4:8, :],
                                          op=ALU.max))
                step(vector.tensor_tensor(ggm[:, 2:4, :], rt_t[:, 0:2, :],
                                          rt_t[:, 2:4, :], op=ALU.max))
                step(vector.tensor_tensor(ggm[:, 0:1, :], ggm[:, 2:3, :],
                                          ggm[:, 3:4, :], op=ALU.max))
                step(vector.tensor_tensor(wtok[:], gg[:],
                                          ggm[:, 0:1, :].broadcast_to([P, 8, 8]),
                                          op=ALU.is_equal))
                step(vector.tensor_scalar(rt_t[:], wtok[:, 0:4, :], -1e30, None,
                                          op0=ALU.mult))
                step(vector.tensor_tensor(rt_t[:], gg[:, 0:4, :], rt_t[:], op=ALU.add))
                step(vector.tensor_scalar(ggm[:, 2:4, :], wtok[:, 4:6, :], -1e30, None,
                                          op0=ALU.mult))
                step(vector.tensor_tensor(ggm[:, 2:4, :], gg[:, 4:6, :], ggm[:, 2:4, :],
                                          op=ALU.add))
                step(vector.tensor_scalar(gg[:, 0:2, :], wtok[:, 6:8, :], -1e30, None,
                                          op0=ALU.mult))
                step(vector.tensor_tensor(gg[:, 0:2, :], gg[:, 6:8, :], gg[:, 0:2, :],
                                          op=ALU.add))
                step(vector.tensor_tensor(rt_t[:, 0:2, :], rt_t[:, 0:2, :],
                                          rt_t[:, 2:4, :], op=ALU.max))
                step(vector.tensor_tensor(rt_t[:, 0:2, :], rt_t[:, 0:2, :],
                                          ggm[:, 2:4, :], op=ALU.max))
                step(vector.tensor_tensor(rt_t[:, 0:2, :], rt_t[:, 0:2, :],
                                          gg[:, 0:2, :], op=ALU.max))
                step(vector.tensor_tensor(ggm[:, 1:2, :], rt_t[:, 0:1, :],
                                          rt_t[:, 1:2, :], op=ALU.max))
                step(vector.tensor_tensor(ggm[:, 2:3, :], ggm[:, 0:1, :],
                                          ggm[:, 1:2, :], op=ALU.subtract))
            VE(f)
            dfd = s_dve.n

            def f(scalar, dfd=dfd):
                scalar.wait_ge(s_dve.h, dfd)
                s_act.inc(scalar.activation(ggm[:, 3, :], ggm[:, 2, :], AF.Sigmoid))
            SC(f)
            p1d = s_act.n

            def f(vector, p1d=p1d):
                vector.wait_ge(s_act.h, p1d)
                need(vector, "gg2")

                def step(instr):
                    s_dve.inc(instr)
                    vector.wait_ge(s_dve.h, s_dve.n)
                step(vector.tensor_copy(gg[:], gt_t[:, :, 0:8].transpose([0, 2, 1])))
                step(vector.tensor_tensor(wtok[:], gg[:],
                                          ggm[:, 0:1, :].broadcast_to([P, 8, 8]),
                                          op=ALU.is_equal))
                step(vector.tensor_tensor(gg[:], gg[:],
                                          ggm[:, 1:2, :].broadcast_to([P, 8, 8]),
                                          op=ALU.is_equal))
                p1b = ggm[:, 3:4, :].broadcast_to([P, 8, 8])
                step(vector.tensor_tensor(wtok[:], wtok[:], gg[:], op=ALU.subtract))
                step(vector.tensor_tensor(wtok[:], wtok[:], p1b, op=ALU.mult))
                step(vector.tensor_tensor(wtok[:], wtok[:], gg[:], op=ALU.add))
                step(vector.tensor_copy(gt_t[:, :, 0:8], wtok[:].transpose([0, 2, 1])))
                step(vector.tensor_copy(wtok[:], gt_t[:, :, 0:8]))
            VE(f)
            rel("gt", s_dve)
            rel("gg2", s_dve)
            wtd = s_dve.n
            dbg(f"wtok{tp}", wtok[:])
            dbg(f"gtok2_{tp}", gt_t[:])
            dbg(f"p1_{tp}", rt_t[:])

            wact0 = s_act.n

            def f(tensor, wtd=wtd, wact0=wact0):
                tensor.wait_ge(s_dve.h, wtd)
                for tc in range(DM):
                    need(tensor, f"ps{2 + tc % 2}")
                    if tc >= 2:
                        tensor.wait_ge(s_act.h, wact0 + tc - 1)
                    s_pe.inc(tensor.transpose((ps[2] if tc % 2 == 0 else ps[3])[0:8, 0:P],
                                              wtok[:, tc, :], idn[:]))
            TE(f)
            wfmm = s_pe.n

            def f(scalar, wfmm=wfmm):
                need(scalar, "wfm")
                for tc in range(DM):
                    scalar.wait_ge(s_pe.h, wfmm - DM + tc + 1)
                    s_act.inc(scalar.activation(wfm[:, tc * P:(tc + 1) * P],
                                                (ps[2] if tc % 2 == 0 else ps[3])[0:8, 0:P],
                                                AF.Copy))
            SC(f)
            rel("ps2", s_act)
            rel("ps3", s_act)
            wfmd = s_act.n

            g_full = xc_t
            y_moe = xn_t
            for e in range(NEL):
                def f(tensor, e=e, wfmd=wfmd, wsel_done=wsel_done):
                    tensor.wait_ge(s_act.h, wfmd)
                    tensor.wait_ge(s_dve.h, wsel_done)
                    for tt in range(2):
                        need(tensor, f"ps{2 + tt}")
                        s_pe.inc(tensor.matmul(ps[2 + tt][:, :], wselb[:, e, :],
                                               wfm[:, tt * TI:(tt + 1) * TI],
                                               start=True, stop=True))
                TE(f)
                wbmm = s_pe.n

                def f(scalar, wbmm=wbmm):
                    need(scalar, "zwbc")
                    for tt in range(2):
                        scalar.wait_ge(s_pe.h, wbmm - 1 + tt)
                        s_act.inc(scalar.activation(z_t[:, 2, tt * TI:(tt + 1) * TI],
                                                    ps[2 + tt][:, :], AF.Copy))
                SC(f)
                rel("ps2", s_act)
                rel("ps3", s_act)
                wbcd = s_act.n

                for hc in range(16):
                    pp = hc % 2

                    wc = 0
                    for wi, wt in ((0, w1T), (1, w2T)):
                        def f(sync, e=e, hc=hc, wt=wt):
                            need(sync, "wst")
                            s_dma.inc(sync.dma_start(
                                out=wst[:].rearrange("p (c x) -> p c x", c=DM),
                                in_=wt[e][hc].rearrange("(c p) x -> p c x", p=P)), 16)
                        SY(f)
                        wld = s_dma.n

                        def f(vector, wld=wld, pp=pp, wi=wi):
                            vector.wait_ge(s_dma.h, wld)
                            if wi == 0:
                                need(vector, f"wbf{pp}")
                            s_dve.inc(vector.tensor_scalar(
                                wbf[:, pp, wi * 1024:(wi + 1) * 1024], wst[:], 1.0,
                                None, op0=ALU.mult))
                        VE(f)
                        rel("wst", s_dve)
                        wc = s_dve.n

                    def f(tensor, wc=wc, pp=pp, ln1=ln1):
                        tensor.wait_ge(s_dve.h, max(wc, ln1))
                        w1v = wbf[:, pp, 0:1024].rearrange("p (c x) -> p c x", c=DM)
                        w2v = wbf[:, pp, 1024:2048].rearrange("p (c x) -> p c x", c=DM)
                        for tt in range(2):
                            sl = slice(tt * TI, (tt + 1) * TI)
                            need(tensor, f"ps{0 + tt}")
                            for c in range(DM):
                                tensor.matmul(ps[0 + tt][:, :], w1v[:, c, :], x_t[:, c, sl],
                                              start=(c == 0), stop=(c == DM - 1))
                            need(tensor, f"ps{4 + tt}")
                            for c in range(DM):
                                mm = tensor.matmul(ps[4 + tt][:, :], w2v[:, c, :],
                                                   x_t[:, c, sl],
                                                   start=(c == 0), stop=(c == DM - 1))
                            s_pe.inc(mm)
                    TE(f)
                    rel(f"wbf{pp}", s_pe)
                    hm1 = s_pe.n - 1

                    def f(scalar, hm1=hm1):
                        need(scalar, "zs1")
                        for tt in range(2):
                            scalar.wait_ge(s_pe.h, hm1 + tt)
                            scalar.activation(z_t[:, 0, tt * TI:(tt + 1) * TI],
                                              (ps[0] if tt == 0 else ps[1])[:, :], AF.Silu)
                            s_act.inc(scalar.activation(
                                z_t[:, 1, tt * TI:(tt + 1) * TI],
                                (ps[4] if tt == 0 else ps[5])[:, :], AF.Copy))
                    SC(f)
                    rel("ps0", s_act)
                    rel("ps1", s_act)
                    rel("ps4", s_act)
                    rel("ps5", s_act)
                    hev = s_act.n

                    def f(vector, hev=hev, hc=hc):
                        vector.wait_ge(s_act.h, hev)
                        need(vector, f"gfull{hc}")
                        s_dve.inc(vector.tensor_tensor(g_full[:, hc, :], z_t[:, 0, :],
                                                       z_t[:, 1, :], op=ALU.mult))
                    VE(f)
                    rel("zs1", s_dve)
                    gd = s_dve.n

                    def f(gpsimd, gd=gd, hc=hc, wbcd=wbcd):
                        gpsimd.wait_ge(s_dve.h, gd)
                        gpsimd.wait_ge(s_act.h, wbcd)
                        s_pool.inc(gpsimd.tensor_tensor(g_full[:, hc, :], g_full[:, hc, :],
                                                        z_t[:, 2, :], op=ALU.mult))
                    GP(f)
                rel("zwbc", s_pool)
                gw_done = s_pool.n

                for dm_c in range(DM):
                    pp = dm_c % 2

                    w3c = 0
                    for hf in range(2):
                        def f(sync, e=e, dm_c=dm_c, hf=hf):
                            need(sync, "wst")
                            s_dma.inc(sync.dma_start(
                                out=wst[:].rearrange("p (c x) -> p c x", c=DM),
                                in_=w3T[e][dm_c][hf * 1024:(hf + 1) * 1024]
                                .rearrange("(c p) x -> p c x", p=P)), 16)
                        SY(f)
                        w3ld = s_dma.n

                        def f(vector, w3ld=w3ld, pp=pp, hf=hf):
                            vector.wait_ge(s_dma.h, w3ld)
                            if hf == 0:
                                need(vector, f"wbf{pp}")
                            s_dve.inc(vector.tensor_scalar(
                                wbf[:, pp, hf * 1024:(hf + 1) * 1024], wst[:], 1.0,
                                None, op0=ALU.mult))
                        VE(f)
                        rel("wst", s_dve)
                        w3c = s_dve.n

                    def f(tensor, w3c=w3c, gw_done=gw_done, pp=pp):
                        tensor.wait_ge(s_dve.h, w3c)
                        tensor.wait_ge(s_pool.h, gw_done)
                        wv = wbf[:, pp, :].rearrange("p (c x) -> p c x", c=EC)
                        for tt in range(2):
                            need(tensor, f"ps{(0 if pp == 0 else 4) + tt}")
                            for c in range(EC):
                                mm = tensor.matmul(
                                    ps[(0 if pp == 0 else 4) + tt][:, :], wv[:, c, :],
                                    g_full[:, c, tt * TI:(tt + 1) * TI],
                                    start=(c == 0), stop=(c == EC - 1))
                            s_pe.inc(mm)
                    TE(f)
                    rel(f"wbf{pp}", s_pe)
                    y1 = s_pe.n - 1

                    def f(scalar, y1=y1, pp=pp):
                        need(scalar, "ytt")
                        for tt in range(2):
                            scalar.wait_ge(s_pe.h, y1 + tt)
                            s_act.inc(scalar.activation(
                                yt_t[:, tt, :],
                                (ps[0 + tt] if pp == 0 else ps[4 + tt])[:, :], AF.Copy))
                    SC(f)
                    rel(f"ps{0 if pp == 0 else 4}", s_act)
                    rel(f"ps{1 if pp == 0 else 5}", s_act)
                    yev = s_act.n

                    def f(vector, yev=yev, dm_c=dm_c, e=e, hc_last=(e == NE - 1)):
                        vector.wait_ge(s_act.h, yev)
                        for tt in range(2):
                            sl = slice(tt * TI, (tt + 1) * TI)
                            if e == 0:
                                mm = vector.tensor_scalar(y_moe[:, dm_c, sl],
                                                          yt_t[:, tt, :], 1.0, None,
                                                          op0=ALU.mult)
                            else:
                                mm = vector.tensor_tensor(y_moe[:, dm_c, sl],
                                                          y_moe[:, dm_c, sl],
                                                          yt_t[:, tt, :], op=ALU.add)
                            s_dve.inc(mm)
                    VE(f)
                    rel("ytt", s_dve)
                for hc in range(16):
                    rel(f"gfull{hc}", s_pe)
            rel("wfm", s_pe)
            moe_done = s_dve.n

            def f(gpsimd, moe_done=moe_done):
                gpsimd.wait_ge(s_dve.h, moe_done)
                gpsimd.dma_start(
                    out=cc_ym_i, in_=y_moe[:].rearrange("p a t -> p (a t)")
                ).then_inc(s_cc.h, 16)
                s_cc.n += 16
                gpsimd.wait_ge(s_cc.h, s_cc.n)
                s_cc.inc(gpsimd.collective_compute(
                    "AllReduce", ALU.add, replica_groups=RG,
                    ins=[cc_ym_i.opt()], outs=[cc_ym_o.opt()]))
                gpsimd.wait_ge(s_cc.h, s_cc.n)
                gpsimd.dma_start(
                    out=y_moe[:].rearrange("p a t -> p (a t)"),
                    in_=cc_ym_o).then_inc(s_cc.h, 16)
                s_cc.n += 16
            GP(f)
            ym_cc = s_cc.n

            def f(vector, ym_cc=ym_cc):
                vector.wait_ge(s_cc.h, ym_cc)
                for c in range(DM):
                    s_dve.inc(vector.tensor_tensor(x_t[:, c, :], x_t[:, c, :],
                                                   y_moe[:, c, :], op=ALU.add))
            VE(f)
            dbg(f"moe{tp}", x_t[:])

            def out_writer(c, tt, pp, dval, tp=tp):
                def f(gpsimd, c=c, tt=tt, pp=pp, dval=dval, tp=tp):
                    gpsimd.wait_ge(s_dve.h, dval)
                    s_ou[pp].inc(gpsimd.dma_start(
                        out=y_hbm[c * P:(c + 1) * P,
                                  tp * TH + tt * TI:tp * TH + (tt + 1) * TI],
                        in_=ost[:, pp, :]), 16)
                GP(f)
                free[f"ostw{pp}"] = [(s_ou[pp], s_ou[pp].n)]
                free[f"ostr{pp}"] = [(s_ou[pp], s_ou[pp].n)]

            xch = [x_t[:, c, :] for c in range(DM)]
            layernorm(xch, 0, False, xch, to_ost=True, out_writer=out_writer)

        def f(gpsimd):
            gpsimd.wait_ge(s_ou[0].h, s_ou[0].n)
            gpsimd.wait_ge(s_ou[1].h, s_ou[1].n)
            gpsimd.wait_ge(s_tap.h, s_tap.n)
        GP(f)

    return nc, sorted(tap_seen)



def _prep_weights(inp, half):
    out = {}
    lo, hi = half * 1024, half * 1024 + 1024
    for l in range(NL):
        iwT = np.ascontiguousarray(inp["L0_attn_in_proj_w"][l].astype(np.float32).T)
        blk = iwT.reshape(D_MODEL, 32, 128).transpose(1, 0, 2)
        sel_blocks = list(range(half * 8, half * 8 + 8)) + \
                     list(range(16 + half * 8, 16 + half * 8 + 8))
        out[f"in_wT{l}"] = np.ascontiguousarray(blk[sel_blocks])
        owT = np.ascontiguousarray(inp["L0_attn_out_proj_w"][l].astype(np.float32).T)
        out[f"out_wT{l}"] = np.ascontiguousarray(
            owT.reshape(D_INNER, DM, 128).transpose(1, 0, 2))
        xpT = np.ascontiguousarray(inp["L0_attn_x_proj_w"][l].astype(np.float32).T)
        out[f"xp_wT{l}"] = np.ascontiguousarray(xpT.reshape(EC, 128, 96)[half * 8:half * 8 + 8])
        dwT = np.ascontiguousarray(inp["L0_attn_dt_proj_w"][l].astype(np.float32).T)
        out[f"dt_wT{l}"] = np.ascontiguousarray(
            dwT.reshape(R, EC, 128).transpose(1, 0, 2)[half * 8:half * 8 + 8])
        sp = np.zeros((P, 384), np.float32)
        sp[:, 0:8] = inp["L0_attn_ln_w"][l].astype(np.float32).reshape(DM, P).T
        sp[:, 8:16] = inp["L0_attn_ln_b"][l].astype(np.float32).reshape(DM, P).T
        cw = inp["L0_attn_conv_w"][l].astype(np.float32).reshape(D_INNER, 4)
        sp[:, 16:48] = cw[lo:hi].reshape(ECL, P, 4).transpose(1, 0, 2).reshape(P, 32)
        sp[:, 48:56] = inp["L0_attn_conv_b"][l].astype(np.float32)[lo:hi].reshape(ECL, P).T
        sp[:, 56:64] = inp["L0_attn_dt_proj_b"][l].astype(np.float32)[lo:hi].reshape(ECL, P).T
        sp[:, 64:72] = inp["L0_attn_D_skip"][l].astype(np.float32)[lo:hi].reshape(ECL, P).T
        al = inp["L0_attn_A_log"][l].astype(np.float32)[lo:hi].reshape(ECL, P, NST)
        sp[:, 72:200] = al.transpose(1, 0, 2).reshape(P, 128)
        out[f"small{l}"] = sp
    out["wgT"] = np.ascontiguousarray(inp["L0_ffn_Wg"].astype(np.float32).T)
    ws = np.zeros((8, NEL, 128), np.float32)
    for j in range(NEL):
        ws[half * NEL + j, j, :] = 1.0
    out["wsel"] = np.ascontiguousarray(ws.reshape(8, NEL * 128))
    for j in range(NEL):
        e = half * NEL + j
        w1 = np.ascontiguousarray(inp["L0_ffn_W1"][e].astype(np.float32).T)
        out[f"w1T{j}"] = np.ascontiguousarray(w1.reshape(D_MODEL, 16, 128).transpose(1, 0, 2))
        w2 = np.ascontiguousarray(inp["L0_ffn_W2"][e].astype(np.float32).T)
        out[f"w2T{j}"] = np.ascontiguousarray(w2.reshape(D_MODEL, 16, 128).transpose(1, 0, 2))
        w3 = np.ascontiguousarray(inp["L0_ffn_W3"][e].astype(np.float32).T)
        out[f"w3T{j}"] = np.ascontiguousarray(w3.reshape(HD, DM, 128).transpose(1, 0, 2))
    return out


def kernel(**inputs):
    B, S, D = inputs["L0_attn_x"].shape
    n_tp = S // TH
    nc, _ = build(n_tp)
    whalf = [_prep_weights(inputs, 0), _prep_weights(inputs, 1)]
    in_maps = []
    for core in range(8):
        b, half = core // 2, core % 2
        m = dict(whalf[half])
        m["x_hbm"] = np.ascontiguousarray(inputs["L0_attn_x"][b].astype(np.float32).T)
        in_maps.append(m)
    res = run_bass_kernel_spmd(nc, in_maps, core_ids=list(range(8)))
    outs = [np.ascontiguousarray(res.results[2 * b]["y_hbm"].T) for b in range(B)]
    return np.stack(outs, axis=0).astype(np.float32)
